# version 13
# baseline (speedup 1.0000x reference)
import numpy as np
import ml_dtypes
import concourse.bass as bass
import concourse.mybir as mybir
from concourse.bass_utils import run_bass_kernel_spmd

F32 = mybir.dt.float32
BF16 = mybir.dt.bfloat16
AF = mybir.ActivationFunctionType
ALU = mybir.AluOpType
AX = mybir.AxisListType

NCORES = 8
HID = 2048
PROJ = 8512
INTER = 4096
NFT = 67
HEADS = 64
Q = 128
SC = 512
NCH = SC // Q
NSC = 2048 // SC
NSEQ = 2
NG = 8
FFI = 8192
EPS = 1e-5


class Sem:
    def __init__(self, ctx, nc, name):
        self.h = ctx.enter_context(nc.semaphore(name))
        self.val = 0

    def tick(self, k=1):
        self.val += k
        return self.val


class Prog:
    def __init__(self):
        self.q = {k: [] for k in ("sync", "gpsimd", "scalar", "vector", "tensor")}

    def wait(self, eng, sem, val):
        if val and val > 0:
            self.q[eng].append(lambda e, s=sem.h, v=val: e.wait_ge(s, v))

    def op(self, eng, fn, sem=None, k=1, serialize=True):
        if sem is None:
            self.q[eng].append(fn)
            return None
        if k == 16 and serialize:
            self.wait(eng, sem, sem.val)
        t = sem.tick(k)
        self.q[eng].append(lambda e, f=fn, s=sem.h, kk=k: f(e).then_inc(s, kk))
        return t


def build_nc(n_steps=None, phase_limit="E", debug=False):
    import contextlib
    nc = bass.Bass("TRN2", target_bir_lowering=False)
    ctx = contextlib.ExitStack()
    PH = "ABCDE".index(phase_limit)
    kind_dbg = "ExternalOutput" if debug else "Internal"


    x_in = nc.dram_tensor("x_in", [NSEQ, 2048, HID], F32, kind="ExternalInput").ap()
    w1 = nc.dram_tensor("w1", [HID, PROJ], F32, kind="ExternalInput").ap()
    convw_i = nc.dram_tensor("convw_i", [4352, 4], F32, kind="ExternalInput").ap()
    convb_i = nc.dram_tensor("convb_i", [34, 128], F32, kind="ExternalInput").ap()
    alog_i = nc.dram_tensor("alog_i", [1, 64], F32, kind="ExternalInput").ap()
    dvec_i = nc.dram_tensor("dvec_i", [1, 64], F32, kind="ExternalInput").ap()
    dtb_i = nc.dram_tensor("dtb_i", [1, 64], F32, kind="ExternalInput").ap()
    normw_i = nc.dram_tensor("normw_i", [32, 128], F32, kind="ExternalInput").ap()
    wout = nc.dram_tensor("wout", [INTER, HID], F32, kind="ExternalInput").ap()
    wgu = nc.dram_tensor("wgu", [2 * FFI, HID], F32, kind="ExternalInput").ap()
    wdn = nc.dram_tensor("wdn", [HID, FFI], F32, kind="ExternalInput").ap()
    idf_i = nc.dram_tensor("idf_i", [128, 128], F32, kind="ExternalInput").ap()
    idb_i = nc.dram_tensor("idb_i", [128, 128], BF16, kind="ExternalInput").ap()
    umask_i = nc.dram_tensor("umask_i", [128, 128], F32, kind="ExternalInput").ap()

    y_out = nc.dram_tensor("y_out", [NSEQ, 2048, HID], F32, kind="ExternalOutput").ap()

    w1b = nc.dram_tensor("w1b", [HID, PROJ], BF16).ap()
    woutb = nc.dram_tensor("woutb", [INTER, HID], BF16).ap()
    wgub = nc.dram_tensor("wgub", [2 * FFI, HID], BF16).ap()
    wgutc = nc.dram_tensor("wgutc", [128, 128, 16, 128], BF16).ap()
    wdb = nc.dram_tensor("wdb", [HID, FFI], BF16).ap()
    wdtc = nc.dram_tensor("wdtc", [16, 128, 64, 128], BF16).ap()
    xbd = nc.dram_tensor("xbd", [INTER, SC], BF16, kind=kind_dbg).ap()
    szd = nc.dram_tensor("szd", [INTER, SC], BF16, kind=kind_dbg).ap()
    ybdT = nc.dram_tensor("ybdT", [32, 128, NCH, 128], BF16).ap()
    x1b = nc.dram_tensor("x1b", [SC, HID], F32, kind=kind_dbg).ap()
    ffbd = nc.dram_tensor("ffbd", [HID, SC], BF16, kind=kind_dbg).ap()
    gsc3 = nc.dram_tensor("gsc3", [NCH, 8192], F32).ap()
    gamdF = nc.dram_tensor("gamdF", [NCH, 64], F32).ap()

    def T(name, shape, dt=F32):
        return ctx.enter_context(nc.sbuf_tensor(name, shape, dt))[:]

    c_idf = T("c_idf", [128, 128])
    c_idb = T("c_idb", [128, 128], BF16)
    c_umask = T("c_umask", [128, 128])
    c_convw = T("c_convw", [128, 34, 4])
    c_convb = T("c_convb", [128, 34])
    c_normw = T("c_normw", [128, 32])
    c_dbc = T("c_dbc", [128, 64])
    c_ones64 = T("c_ones64", [64, 1])
    c_acol = T("c_acol", [64, 1])
    c_eacol = T("c_eacol", [64, 1])
    c_dtb = T("c_dtb", [64, 1])
    st_rows = T("st_rows", [34, 128])
    s64 = T("s64", [64, 4])

    Hb = T("Hb", [128, INTER])
    Hbf = T("Hbf", [128, INTER], BF16)
    convh = T("convh", [128, NSEQ, 34, 3])
    colT = T("colT", [128, 16, SC], BF16)
    Bb = T("Bb", [128, SC], BF16)
    Cb = T("Cb", [128, SC], BF16)
    BTb = T("BTb", [128, 128], BF16)
    dtF = T("dtF", [64, SC])

    arA = T("arA", [128, 18432])
    arB = T("arB", [128, 7168])
    arC = T("arC", [128, 6144])

    prepP1 = T("prepP1", [128, 512])
    prepP2 = T("prepP2", [128, 512])
    GinF = prepP1[0:64, :]
    cumF = prepP1[64:128, :]
    dEndArgF = prepP2[0:64, :]
    GT = T("GT", [128, 64])
    dInT = T("dInT", [128, 64])
    dEndT = T("dEndT", [128, 64])
    dtT = T("dtT", [128, 64])
    gby = T("gby", [128, 64])
    CBm = T("CBm", [128, 128])
    mtb = T("mtb", [128, 2, 8, 128], BF16)
    xdt = T("xdt", [128, 2, 8, 64], BF16)
    xwv = T("xwv", [128, 2, 8, 64], BF16)
    dxt = T("dxt", [128, 2, 8, 64])

    xnst = T("xnst", [128, 2, HID], BF16)
    xsil = T("xsil", [128, 2, SC], BF16)
    glraw = T("glraw", [128, 4, SC], BF16)
    szw = T("szw", [128, 4, SC], BF16)
    smalls = T("smalls", [128, 16])

    psA = ctx.enter_context(nc.psum_tensor("psA", [128, 2048], F32))[:]
    psB = ctx.enter_context(nc.psum_tensor("psB", [128, 2048], F32))[:]

    hbuf = arA[:, 0:16384].bitcast(BF16).rearrange("p (i t) -> p i t", i=64)
    Ybuf = arA[:, 0:4096]
    XTc = arA[:, 4096:8192].bitcast(BF16).rearrange("p (c t) -> p c t", c=2)
    stag = [arA[:, 16384:17408].bitcast(BF16).rearrange("p (j n) -> p j n", j=16),
            arA[:, 17408:18432].bitcast(BF16).rearrange("p (j n) -> p j n", j=16)]
    ybtstg = arA[:, 8192:10240].bitcast(BF16)
    aosb = arA[:, 8192:16384]
    sgst = arB[:, 6144:7168].rearrange("p (a t) -> p a t", a=2)
    cacc = arB[:, 2048:3072].rearrange("p (a t) -> p a t", a=2)
    grpv = [arB[:, 0:1024], arB[:, 1024:2048]]
    dbufv = arB[:, 2048:3072]
    dminv = [arB[:, 3072:4096], arB[:, 4096:5120]]
    expdv = [arB[:, 5120:6144], arB[:, 6144:7168]]
    sqdump = arB[:, 0:4096]
    ysb = arB[:, 4096:6144].bitcast(BF16)
    xtileA = arB[:, 0:2048]
    xtileB = arB[:, 2048:4096]
    lndump = arB[:, 4096:6144]
    outstg = arB[:, 4096:6144]
    dt_t1 = arB[:, 0:512][0:64, :]
    dt_t2 = arB[:, 512:1024][0:64, :]
    dt_t3 = arB[:, 1024:1536][0:64, :]

    w1wF = arC[:, 0:4096].bitcast(BF16)
    w1w4 = w1wF.rearrange("p (q k n) -> p q k n", q=4, k=16)
    wow4v = arC[:, 2048:4096].bitcast(BF16).rearrange("p (j n) -> p j n", j=4)
    wguw = [[arC[:, 0:1024].bitcast(BF16).rearrange("p (k n) -> p k n", k=16),
             arC[:, 1024:2048].bitcast(BF16).rearrange("p (k n) -> p k n", k=16)],
            [arC[:, 2048:3072].bitcast(BF16).rearrange("p (k n) -> p k n", k=16),
             arC[:, 3072:4096].bitcast(BF16).rearrange("p (k n) -> p k n", k=16)]]
    wdw = [arC[:, 4096:5120].bitcast(BF16).rearrange("p (i n) -> p i n", i=16),
           arC[:, 5120:6144].bitcast(BF16).rearrange("p (i n) -> p i n", i=16)]
    wguwF = [[arC[:, 0:1024].bitcast(BF16), arC[:, 1024:2048].bitcast(BF16)],
             [arC[:, 2048:3072].bitcast(BF16), arC[:, 3072:4096].bitcast(BF16)]]
    wdwF = [arC[:, 4096:5120].bitcast(BF16), arC[:, 5120:6144].bitcast(BF16)]
    fftl = [arC[:, 0:1024].bitcast(BF16), arC[:, 1024:2048].bitcast(BF16)]
    xbcp = arB[:, 3072:4102].rearrange("p (a t) -> p a t", a=2)
    ffst = arB[:, 6144:7168].bitcast(BF16).rearrange("p (a t) -> p a t", a=2)[:, :, 0:512]

    lnsum = smalls[:, 0:1]
    lnm = smalls[:, 1:2]
    lnmsq = smalls[:, 2:3]
    lnvar = smalls[:, 3:4]
    lnln = smalls[:, 4:5]
    lnrstd = smalls[:, 5:6]
    ssq = smalls[:, 6:7]
    grstd = smalls[:, 7:8]

    def bankA(i):
        return psA[:, i * 512:(i + 1) * 512]

    def bankB(i):
        return psB[:, i * 512:(i + 1) * 512]

    ps_cbt = psB[:, 1024:1152]
    ps_gt = psB[:, 1152:1216]
    ps_dint = psB[:, 1216:1280]
    ps_dendt = psB[:, 1280:1344]
    ps_dtt = psB[:, 1344:1408]
    ps_bt = psB[:, 1536:1600].bitcast(BF16)
    psTB = [psB[:, 1024:1536].bitcast(BF16), psB[:, 1536:2048].bitcast(BF16)]

    P = Prog()
    sWw = Sem(ctx, nc, "sWw")
    sWt = Sem(ctx, nc, "sWt")
    sGa = Sem(ctx, nc, "sGa")
    sGo = Sem(ctx, nc, "sGo")
    sGr = Sem(ctx, nc, "sGr")
    sGs = Sem(ctx, nc, "sGs")
    sT = Sem(ctx, nc, "sT")
    sV = Sem(ctx, nc, "sV")
    sS = Sem(ctx, nc, "sS")
    sCa = Sem(ctx, nc, "sCa")
    sCo = Sem(ctx, nc, "sCo")
    sCb = Sem(ctx, nc, "sCb")

    bank_free = {}

    def wait_bank(eng, *keys):
        for key in keys:
            if key in bank_free:
                sem, val = bank_free.pop(key)
                P.wait(eng, sem, val)

    tb_rot = [0]

    def tr_batch(srcs, copy_eng, copy_sem, dst2d=None, dst3d=None, pre_cp=()):
        bank = tb_rot[0]
        tb_rot[0] ^= 1
        wait_bank("tensor", "B%d" % (2 + bank))
        last_tr = None
        for idx, s0 in enumerate(srcs):
            fn = (lambda e, s0=s0, idx=idx, bank=bank: e.transpose(
                out=psTB[bank][:, idx * 128:(idx + 1) * 128], in_=s0, identity=c_idb))
            last_tr = P.op("tensor", fn, sT) if idx == len(srcs) - 1 else P.op("tensor", fn)
        for (weng, wsem, wval) in pre_cp:
            P.wait(copy_eng, wsem, wval)
        P.wait(copy_eng, sT, last_tr)
        n = len(srcs)
        if copy_eng == "scalar":
            tcp = P.op("scalar", lambda e, bank=bank, n=n, dst2d=dst2d: e.activation(
                out=dst2d, in_=psTB[bank][:, 0:n * 128], func=AF.Copy), copy_sem)
        elif dst2d is not None:
            tcp = P.op("vector", lambda e, bank=bank, n=n, dst2d=dst2d: e.tensor_copy(
                out=dst2d, in_=psTB[bank][:, 0:n * 128]), copy_sem)
        else:
            src_ap = psTB[bank][:, 0:n * 128].rearrange("p (j n) -> p j n", j=n)
            tcp = P.op("vector", lambda e, dst3d=dst3d, src_ap=src_ap: e.tensor_copy(
                out=dst3d, in_=src_ap), copy_sem)
        bank_free["B%d" % (2 + bank)] = (copy_sem, tcp)
        return last_tr, tcp

    for dst, src in ((c_idf, idf_i), (c_idb, idb_i), (c_umask, umask_i)):
        P.op("gpsimd", lambda e, d=dst, s=src: e.dma_start(out=d, in_=s), sGs, 16)
    P.op("gpsimd", lambda e: e.dma_start(
        out=c_convw, in_=convw_i.rearrange("(j p) k -> p j k", p=128)), sGs, 16)
    P.op("gpsimd", lambda e: e.dma_start(out=st_rows, in_=convb_i), sGs, 16)
    t_cb_rows = sGs.val
    P.op("gpsimd", lambda e: e.dma_start(
        out=c_dbc, in_=dvec_i.partition_broadcast(128).squeeze(1)), sGs, 16)
    P.op("gpsimd", lambda e: e.dma_start(out=s64[0:64, 0:1], in_=alog_i.transpose([1, 0])), sGs, 16)
    P.op("gpsimd", lambda e: e.dma_start(out=s64[0:64, 1:2], in_=dtb_i.transpose([1, 0])), sGs, 16)
    t_small_rows = sGs.val
    P.wait("tensor", sGs, t_small_rows)
    t1 = P.op("tensor", lambda e: e.transpose(
        out=psB[:, 0:34], in_=st_rows, identity=c_idf[0:34, 0:34]), sT)
    P.wait("vector", sT, t1)
    t_cb_done = P.op("vector", lambda e: e.tensor_copy(out=c_convb, in_=psB[:, 0:34]), sV)
    P.wait("gpsimd", sV, t_cb_done)
    P.op("gpsimd", lambda e: e.dma_start(out=st_rows[0:32, :], in_=normw_i), sGs, 16)
    t_nw_rows = sGs.val
    P.wait("tensor", sGs, t_nw_rows)
    t2 = P.op("tensor", lambda e: e.transpose(
        out=psB[:, 64:96], in_=st_rows[0:32, :], identity=c_idf[0:32, 0:32]), sT)
    P.wait("vector", sT, t2)
    t_nw_done = P.op("vector", lambda e: e.tensor_copy(out=c_normw, in_=psB[:, 64:96]), sV)
    bank_free["B0"] = (sV, t_nw_done)
    P.wait("scalar", sGs, t_nw_rows)
    t3 = P.op("scalar", lambda e: e.activation(out=c_eacol, in_=s64[0:64, 0:1], func=AF.Exp), sS)
    P.wait("vector", sS, t3)
    P.op("vector", lambda e: e.tensor_scalar_mul(out=c_acol, in0=c_eacol, scalar1=-1.0))
    P.op("vector", lambda e: e.tensor_copy(out=c_dtb, in_=s64[0:64, 1:2]))
    P.op("vector", lambda e: e.memset(c_ones64, 1.0))
    P.op("vector", lambda e: e.memset(Hb, 0.0))
    P.op("vector", lambda e: e.memset(Hbf, 0.0))
    P.op("vector", lambda e: e.memset(convh, 0.0), sV)

    W1CH = [(0, 9), (9, 18), (18, 27), (27, 36), (36, 45), (45, 54), (54, 63), (63, 67)]

    def emit_w1_chunk(k):
        f0, f1 = W1CH[k]
        c0, c1 = f0 * 128, min(f1 * 128, PROJ)
        P.op("gpsimd", lambda e, c0=c0, c1=c1: e.dma_start(
            out=w1b[:, c0:c1], in_=w1[:, c0:c1]), sCa, 16)

    def emit_rest_casts():
        for i in range(4):
            P.op("gpsimd", lambda e, i=i: e.dma_start(
                out=woutb[i * 1024:(i + 1) * 1024, :], in_=wout[i * 1024:(i + 1) * 1024, :]),
                sCo, 16, serialize=False)
        for i in range(8):
            P.op("gpsimd", lambda e, i=i: e.dma_start(
                out=wgub[i * 2048:(i + 1) * 2048, :], in_=wgu[i * 2048:(i + 1) * 2048, :]),
                sCb, 16, serialize=False)
        for i in range(4):
            P.op("gpsimd", lambda e, i=i: e.dma_start(
                out=wdb[:, i * 2048:(i + 1) * 2048], in_=wdn[:, i * 2048:(i + 1) * 2048]),
                sCb, 16, serialize=False)

    t_w1_cast = 16 * len(W1CH)
    t_wout_cast = 4 * 16
    t_wgud_cast = 12 * 16

    sp_v0 = smalls[:, 12:13]
    sp_v1 = smalls[:, 13:14]
    sp_s0 = smalls[:, 14:15]
    sp_s1 = smalls[:, 15:16]

    def v_spacer():
        P.op("vector", lambda e: e.tensor_copy(out=sp_v0, in_=sp_v1))

    def s_spacer():
        P.op("scalar", lambda e: e.activation(out=sp_s0, in_=sp_s1, func=AF.Identity))

    def ln_stats(xtile, F):
        P.op("vector", lambda e, xtile=xtile: e.reduce_sum(out=lnsum, in_=xtile, axis=AX.X))
        t2 = P.op("vector", lambda e, xtile=xtile, F=F: e.scalar_tensor_tensor(
            out=lndump[:, 0:F], in0=xtile, scalar=1.0, in1=xtile,
            op0=ALU.mult, op1=ALU.mult, accum_out=ssq), sV)
        P.wait("vector", sV, t2)
        P.op("vector", lambda e, F=F: e.tensor_scalar_mul(out=lnm, in0=lnsum, scalar1=1.0 / F))
        t3 = P.op("vector", lambda e, F=F: e.tensor_scalar(
            out=lnvar, in0=ssq, scalar1=1.0 / F, scalar2=EPS, op0=ALU.mult, op1=ALU.add), sV)
        P.wait("vector", sV, t3)
        tm = P.op("vector", lambda e: e.tensor_mul(out=lnmsq, in0=lnm, in1=lnm), sV)
        P.wait("vector", sV, tm)
        t = P.op("vector", lambda e: e.tensor_sub(out=lnvar, in0=lnvar, in1=lnmsq), sV)
        P.wait("scalar", sV, t)
        ts5 = P.op("scalar", lambda e: e.activation(out=lnln, in_=lnvar, func=AF.Ln), sS)
        P.wait("scalar", sS, ts5)
        return P.op("scalar", lambda e: e.activation(
            out=lnrstd, in_=lnln, func=AF.Exp, scale=-0.5), sS)

    t_colT_consumed = 0
    t_arB_v = sV.val
    t_xt_reads = 0
    t_yb_reads = 0
    t_sz_reads = 0
    t_x1_reads = 0
    t_aob_reads = 0
    t_hbuf_cons = 0
    t_mmfin = 0
    t_opsw_last = 0
    t_gating_v = 0
    t_dt = 0
    t_yout_last = 0
    pc = {}
    t_cap_gu_done = 0
    t_cap_dn_done = 0

    step = 0
    stop_all = False
    for s in range(NSEQ):
        if stop_all:
            break
        if s > 0:
            P.wait("vector", sT, sT.val)
            P.op("vector", lambda e: e.memset(Hb, 0.0))
            tmr = P.op("vector", lambda e: e.memset(Hbf, 0.0), sV)
        for sc in range(NSC):
            if n_steps is not None and step >= n_steps:
                stop_all = True
                break
            step += 1
            last_step = (n_steps is not None and step == n_steps)
            t0 = sc * SC
            first = (s == 0 and sc == 0)
            tr_hist = {}
            for tt in range(4):
                xt = xtileA if tt % 2 == 0 else xtileB
                P.wait("gpsimd", sV, t_arB_v)
                if first:
                    emit_w1_chunk(2 * tt)
                tl = P.op("gpsimd", lambda e, s=s, r0=t0 + tt * 128, xt=xt: e.dma_start(
                    out=xt, in_=x_in[s, r0:r0 + 128, :]), sGa, 16)
                if first:
                    emit_w1_chunk(2 * tt + 1)
                    if tt == 3:
                        emit_rest_casts()
                P.wait("vector", sGa, tl)
                if tt >= 2:
                    P.wait("vector", sT, tr_hist[tt - 2])
                if tt == 0:
                    P.wait("vector", sGo, t_yout_last)
                trs = ln_stats(xt, HID)
                P.wait("vector", sS, trs)
                tn = P.op("vector", lambda e, xt=xt, tt=tt: e.tensor_scalar(
                    out=xnst[:, tt % 2, :], in0=xt, scalar1=lnm, scalar2=lnrstd,
                    op0=ALU.subtract, op1=ALU.mult), sV)
                t_arB_v = tn
                P.wait("tensor", sV, tn)
                for r in range(2):
                    srcs = [xnst[:, tt % 2, (r * 8 + k) * 128:(r * 8 + k + 1) * 128]
                            for k in range(8)]
                    pre_cp = []
                    if tt == 0 and r == 0:
                        pre_cp.append(("vector", sT, t_colT_consumed))
                    ltr, t_colT_ready = tr_batch(
                        srcs, "vector", sV,
                        dst3d=colT[:, r * 8:(r + 1) * 8, tt * 128:(tt + 1) * 128],
                        pre_cp=pre_cp)
                tr_hist[tt] = ltr

            if last_step and PH < 1:
                continue
            t_mm = {}
            t_xsil_dma = {0: 0, 1: 0}
            t_conv_v = {0: 0, 1: 0}
            P.wait("gpsimd", sWt, t_xt_reads)
            P.wait("gpsimd", sWw, t_sz_reads)
            for f in range(NFT):
                w = 128 if f < NFT - 1 else 64
                q = f % 4
                if f >= 4:
                    P.wait("sync", sT, t_mm[f - 4])
                if f == 0:
                    P.wait("sync", sGo, t_opsw_last)
                    P.wait("sync", sV, t_arB_v)
                if first:
                    P.wait("sync", sCa, 16 * min(f // 9 + 1, 8))
                tload = P.op("sync", lambda e, f=f, w=w, q=q: e.dma_start(
                    out=w1w4[:, q, :, 0:w],
                    in_=w1b[:, f * 128:f * 128 + w].rearrange("(k p) n -> p k n", p=128)),
                    sWw, 16)
                P.wait("tensor", sWw, tload)
                if f == 0:
                    P.wait("tensor", sV, t_colT_ready)
                wait_bank("tensor", "A%d" % q)
                for k in range(16):
                    fn = (lambda e, q=q, k=k, w=w: e.matmul(
                        out=bankA(q)[0:w, :], lhsT=w1w4[:, q, k, 0:w],
                        rhs=colT[:, k, :], start=(k == 0), stop=(k == 15)))
                    if k == 15:
                        t_mm[f] = P.op("tensor", fn, sT)
                    else:
                        P.op("tensor", fn)
                P.wait("scalar", sT, t_mm[f])
                if f < 32:
                    P.wait("scalar", sGo, t_xsil_dma[f % 2])
                    tz = P.op("scalar", lambda e, f=f: e.activation(
                        out=xsil[:, f % 2, :], in_=bankA(f % 4), func=AF.Silu, scale=0.25), sS)
                    bank_free["A%d" % (f % 4)] = (sS, tz)
                    P.wait("gpsimd", sS, tz)
                    t_xsil_dma[f % 2] = P.op("gpsimd", lambda e, f=f: e.dma_start(
                        out=szd[f * 128:(f + 1) * 128, :], in_=xsil[:, f % 2, :]), sGo, 16)
                elif f < 66:
                    ct = f - 32
                    P.wait("scalar", sV, t_conv_v[f % 2])
                    tc = P.op("scalar", lambda e, f=f: e.activation(
                        out=xbcp[:, f % 2, 3:515], in_=bankA(f % 4), func=AF.Copy, scale=0.5), sS)
                    bank_free["A%d" % (f % 4)] = (sS, tc)
                    P.wait("vector", sS, tc)
                    th0 = P.op("vector", lambda e, s=s, ct=ct, f=f: e.tensor_copy(
                        out=xbcp[:, f % 2, 0:3], in_=convh[:, s, ct, :]), sV)
                    P.wait("vector", sV, th0)
                    tcv0 = P.op("vector", lambda e, f=f, ct=ct: e.tensor_scalar_mul(
                        out=cacc[:, f % 2, :], in0=xbcp[:, f % 2, 0:512],
                        scalar1=c_convw[:, ct, 0:1]), sV)
                    for k in range(1, 4):
                        P.wait("vector", sV, tcv0)
                        tcv0 = P.op("vector", lambda e, f=f, ct=ct, k=k: e.scalar_tensor_tensor(
                            out=cacc[:, f % 2, :], in0=xbcp[:, f % 2, k:k + 512],
                            scalar=c_convw[:, ct, k:k + 1], in1=cacc[:, f % 2, :],
                            op0=ALU.mult, op1=ALU.add), sV)
                    P.wait("vector", sV, tcv0)
                    t_conv_v[f % 2] = P.op("vector", lambda e, s=s, ct=ct, f=f: e.tensor_copy(
                        out=convh[:, s, ct, :], in_=xbcp[:, f % 2, 512:515]), sV)
                    P.wait("scalar", sV, t_conv_v[f % 2])
                    if ct < 32:
                        P.wait("scalar", sGo, t_xsil_dma[f % 2])
                        tsl = P.op("scalar", lambda e, f=f, ct=ct: e.activation(
                            out=xsil[:, f % 2, :], in_=cacc[:, f % 2, :], func=AF.Silu,
                            bias=c_convb[:, ct:ct + 1]), sS)
                        P.wait("gpsimd", sS, tsl)
                        t_xsil_dma[f % 2] = P.op("gpsimd", lambda e, ct=ct, f=f: e.dma_start(
                            out=xbd[ct * 128:(ct + 1) * 128, :], in_=xsil[:, f % 2, :]), sGo, 16)
                    elif ct == 32:
                        P.wait("scalar", sT, pc.get("gmm_last", 0))
                        P.op("scalar", lambda e, f=f, ct=ct: e.activation(
                            out=Bb, in_=cacc[:, f % 2, :], func=AF.Silu,
                            bias=c_convb[:, ct:ct + 1]), sS)
                    else:
                        P.op("scalar", lambda e, f=f, ct=ct: e.activation(
                            out=Cb, in_=cacc[:, f % 2, :], func=AF.Silu,
                            bias=c_convb[:, ct:ct + 1]), sS)
                else:
                    P.wait("vector", sV, t_arB_v)
                    P.wait("vector", sT, t_mm[f])
                    td0 = P.op("vector", lambda e, f=f: e.tensor_scalar_add(
                        out=dt_t1, in0=bankA(f % 4)[0:64, :], scalar1=c_dtb), sV)
                    bank_free["A%d" % (f % 4)] = (sV, td0)
                    P.wait("vector", sV, td0)
                    tda = P.op("vector", lambda e: e.tensor_scalar_mul(
                        out=dt_t2, in0=dt_t1, scalar1=-1.0), sV)
                    P.wait("vector", sV, tda)
                    tv = P.op("vector", lambda e: e.tensor_tensor(
                        out=dt_t2, in0=dt_t1, in1=dt_t2, op=ALU.min), sV)
                    P.wait("scalar", sV, tv)
                    tse = P.op("scalar", lambda e: e.activation(out=dt_t3, in_=dt_t2, func=AF.Exp), sS)
                    P.wait("scalar", sS, tse)
                    ts2 = P.op("scalar", lambda e: e.activation(
                        out=dt_t3, in_=dt_t3, func=AF.Ln, bias=1.0), sS)
                    P.wait("vector", sS, ts2)
                    P.wait("vector", sV, tv)
                    tdb = P.op("vector", lambda e: e.tensor_scalar_max(
                        out=dt_t2, in0=dt_t1, scalar1=0.0), sV)
                    P.wait("vector", sV, tdb)
                    t_dt = P.op("vector", lambda e: e.tensor_add(out=dtF, in0=dt_t2, in1=dt_t3), sV)
                    t_arB_v = t_dt
            t_colT_consumed = t_mm[NFT - 1]
            t_BC_scalar = sS.val
            t_B_gp = sGo.val

            if last_step and PH < 2:
                continue
            P.wait("vector", sT, pc.get("t_cbt", 0))
            P.wait("vector", sGs, pc.get("tgw", 0))
            P.wait("vector", sV, t_dt)
            tscan = P.op("vector", lambda e: e.tensor_tensor_scan(
                out=cumF, data0=c_ones64.broadcast_to([64, 512]), data1=dtF,
                initial=0.0, op0=ALU.mult, op1=ALU.add), sV)
            P.wait("vector", sV, tscan)
            tlast = P.op("vector", lambda e: e.tensor_scalar_mul(
                out=GinF[:, 0:128], in0=cumF[:, 0:128], scalar1=c_acol), sV)
            for c in range(1, NCH):
                tlast = P.op("vector", lambda e, c=c: e.tensor_scalar(
                    out=GinF[:, c * 128:(c + 1) * 128],
                    in0=cumF[:, c * 128:(c + 1) * 128],
                    scalar1=cumF[:, c * 128 - 1:c * 128], scalar2=c_acol,
                    op0=ALU.subtract, op1=ALU.mult), sV)
            t_inarg = tlast
            for c in range(NCH):
                tlast = P.op("vector", lambda e, c=c: e.tensor_scalar(
                    out=dEndArgF[:, c * 128:(c + 1) * 128],
                    in0=cumF[:, c * 128:(c + 1) * 128],
                    scalar1=cumF[:, c * 128 + 127:c * 128 + 128], scalar2=c_eacol,
                    op0=ALU.subtract, op1=ALU.mult), sV)
            t_endarg = tlast
            P.wait("gpsimd", sV, t_inarg)
            tgw = P.op("gpsimd", lambda e: e.dma_start(
                out=gsc3.rearrange("c (h t) -> h c t", t=128),
                in_=GinF.rearrange("h (c t) -> h c t", t=128)), sGs, 16)
            pc["tgw"] = tgw
            for c in range(NCH):
                cp = c % 2
                P.wait("sync", sGo, t_B_gp)
                if c == 0:
                    P.wait("sync", sT, t_hbuf_cons)
                wait_bank("sync", "XT%d" % cp)
                t_xt = P.op("sync", lambda e, c=c, cp=cp: e.dma_start(
                    out=XTc[:, cp, :], in_=xbd[:, c * 128:(c + 1) * 128], transpose=True), sWt, 16)
                t_xt_reads = sWt.val
                wait_bank("tensor", "B2", "B3")
                P.wait("tensor", sV, t_endarg)
                P.op("tensor", lambda e, c=c: e.transpose(
                    out=ps_gt, in_=GinF[:, c * 128:(c + 1) * 128], identity=c_idf[0:64, 0:64]))
                P.op("tensor", lambda e, c=c: e.transpose(
                    out=ps_dendt, in_=dEndArgF[:, c * 128:(c + 1) * 128],
                    identity=c_idf[0:64, 0:64]))
                P.op("tensor", lambda e, c=c: e.transpose(
                    out=ps_dtt, in_=dtF[:, c * 128:(c + 1) * 128], identity=c_idf[0:64, 0:64]))
                P.wait("tensor", sS, t_BC_scalar)
                P.op("tensor", lambda e, c=c: e.transpose(
                    out=ps_bt, in_=Bb[:, c * 128:(c + 1) * 128], identity=c_idb))
                t_cbt = P.op("tensor", lambda e, c=c: e.matmul(
                    out=ps_cbt, lhsT=Bb[:, c * 128:(c + 1) * 128],
                    rhs=Cb[:, c * 128:(c + 1) * 128], start=True, stop=True), sT)
                P.wait("vector", sT, t_cbt)
                P.wait("vector", sT, pc.get("gmm_last", 0))
                P.wait("vector", sS, pc.get("texpi", 0))
                P.op("vector", lambda e: e.tensor_copy(out=GT, in_=ps_gt))
                P.op("vector", lambda e: e.tensor_copy(out=dEndT, in_=ps_dendt))
                P.op("vector", lambda e: e.tensor_copy(out=dtT, in_=ps_dtt))
                P.op("vector", lambda e: e.tensor_copy(out=BTb, in_=ps_bt))
                t_sm = P.op("vector", lambda e: e.tensor_mul(out=CBm, in0=ps_cbt, in1=c_umask), sV)
                bank_free["B2"] = (sV, t_sm)
                bank_free["B3"] = (sV, t_sm)
                P.wait("scalar", sV, t_sm)
                P.wait("scalar", sGs, pc.get("tgam_c", 0))
                texpi = P.op("scalar", lambda e: e.activation(
                    out=dInT, in_=GT, func=AF.Exp), sS)
                pc["texpi"] = texpi
                texpe = P.op("scalar", lambda e: e.activation(
                    out=dEndT, in_=dEndT, func=AF.Exp), sS)
                P.wait("vector", sS, texpe)
                twE = P.op("vector", lambda e: e.tensor_mul(out=dEndT, in0=dEndT, in1=dtT), sV)
                P.wait("gpsimd", sS, texpi)
                P.wait("gpsimd", sV, t_arB_v)
                tgam_c = P.op("gpsimd", lambda e, c=c: e.dma_start(
                    out=gamdF[c:c + 1, :], in_=dInT[127:128, :]), sGs, 16)
                pc["tgam_c"] = tgam_c
                tgby = P.op("gpsimd", lambda e, c=c: e.dma_start(
                    out=gby, in_=gamdF[c:c + 1, :].partition_broadcast(128).squeeze(1)), sGs, 16)
                P.wait("vector", sV, twE)

                t_group_v = {}
                t_gmm = 0
                if c == 0:
                    P.wait("vector", sS, t_BC_scalar)
                    P.wait("vector", sT, t_hbuf_cons)
                P.wait("vector", sT, pc.get("ytr", 0))
                for g in range(NG):
                    gp = g % 2
                    P.wait("gpsimd", sGs, tgby)
                    if g >= 2:
                        P.wait("gpsimd", sV, t_group_v[g - 2])
                    else:
                        P.wait("gpsimd", sV, t_arB_v)
                    t_grp = P.op("gpsimd", lambda e, c=c, g=g, gp=gp: e.dma_start(
                        out=grpv[gp],
                        in_=gsc3[c:c + 1, g * 1024:(g + 1) * 1024]
                        .partition_broadcast(128).squeeze(1)), sGr, 16)
                    P.wait("vector", sGr, t_grp)
                    P.wait("vector", sV, t_sm)
                    tdb0 = P.op("vector", lambda e, gp=gp, g=g: e.tensor_tensor(
                        out=dbufv.rearrange("p (h t) -> p h t", h=8),
                        in0=grpv[gp].rearrange("p (h t) -> p h t", h=8),
                        in1=GT[:, g * 8:(g + 1) * 8].unsqueeze(2).broadcast_to([128, 8, 128]),
                        op=ALU.subtract), sV)
                    P.wait("vector", sV, tdb0)
                    tdm = P.op("vector", lambda e, gp=gp: e.tensor_scalar_min(
                        out=dminv[gp], in0=dbufv, scalar1=0.0), sV)
                    P.wait("scalar", sV, tdm)
                    texp = P.op("scalar", lambda e, gp=gp: e.activation(
                        out=expdv[gp], in_=dminv[gp], func=AF.Exp), sS)
                    P.wait("vector", sS, texp)
                    P.op("vector", lambda e, gp=gp: e.tensor_tensor(
                        out=mtb[:, gp, :, :],
                        in0=expdv[gp].rearrange("p (h t) -> p h t", h=8),
                        in1=CBm.unsqueeze(1).broadcast_to([128, 8, 128]), op=ALU.mult))
                    if g == 0:
                        P.wait("vector", sWt, t_xt)
                    xtv = XTc[:, cp, :].rearrange("p (h q) -> p h q", h=64)
                    P.op("vector", lambda e, gp=gp, g=g, xtv=xtv: e.tensor_tensor(
                        out=xdt[:, gp, :, :], in0=xtv[:, g * 8:(g + 1) * 8, :],
                        in1=dtT[:, g * 8:(g + 1) * 8].unsqueeze(2).broadcast_to([128, 8, 64]),
                        op=ALU.mult))
                    P.op("vector", lambda e, gp=gp, g=g, xtv=xtv: e.tensor_tensor(
                        out=xwv[:, gp, :, :], in0=xtv[:, g * 8:(g + 1) * 8, :],
                        in1=dEndT[:, g * 8:(g + 1) * 8].unsqueeze(2).broadcast_to([128, 8, 64]),
                        op=ALU.mult))
                    tprep = P.op("vector", lambda e, gp=gp, g=g, xtv=xtv: e.tensor_tensor(
                        out=dxt[:, gp, :, :], in0=xtv[:, g * 8:(g + 1) * 8, :],
                        in1=c_dbc[:, g * 8:(g + 1) * 8].unsqueeze(2).broadcast_to([128, 8, 64]),
                        op=ALU.mult), sV)
                    P.wait("tensor", sV, tprep)
                    wait_bank("tensor", "A%d" % gp, "A%d" % (2 + gp), "B%d" % gp)
                    for j in range(8):
                        P.op("tensor", lambda e, c=c, g=g, j=j, gp=gp: e.matmul(
                            out=bankA(2 + gp)[:, j * 64:(j + 1) * 64],
                            lhsT=Cb[:, c * 128:(c + 1) * 128],
                            rhs=Hbf[:, (8 * g + j) * 64:(8 * g + j + 1) * 64],
                            start=(j == 0), stop=(j == 7)))
                    for j in range(8):
                        P.op("tensor", lambda e, g=g, j=j, gp=gp: e.matmul(
                            out=bankA(gp)[:, j * 64:(j + 1) * 64],
                            lhsT=mtb[:, gp, j, :], rhs=xdt[:, gp, j, :],
                            start=(j == 0), stop=(j == 7)))
                    for j in range(8):
                        fn = (lambda e, g=g, j=j, gp=gp: e.matmul(
                            out=bankB(gp)[:, j * 64:(j + 1) * 64],
                            lhsT=BTb, rhs=xwv[:, gp, j, :],
                            start=(j == 0), stop=(j == 7)))
                        if j == 7:
                            t_gmm = P.op("tensor", fn, sT)
                        else:
                            P.op("tensor", fn)
                    P.wait("vector", sT, t_gmm)
                    if g == 0:
                        P.wait("vector", sGs, tgby)
                    ysl = Ybuf[:, g * 512:(g + 1) * 512].rearrange("p (h q) -> p h q", h=8)
                    ty1 = P.op("vector", lambda e, g=g, gp=gp, ysl=ysl: e.tensor_tensor(
                        out=ysl, in0=bankA(2 + gp).rearrange("p (h q) -> p h q", h=8),
                        in1=dInT[:, g * 8:(g + 1) * 8].unsqueeze(2).broadcast_to([128, 8, 64]),
                        op=ALU.mult), sV)
                    P.wait("vector", sV, ty1)
                    ty2 = P.op("vector", lambda e, gp=gp, ysl=ysl: e.tensor_tensor(
                        out=ysl, in0=ysl,
                        in1=bankA(gp).rearrange("p (h q) -> p h q", h=8), op=ALU.add), sV)
                    P.wait("vector", sV, ty2)
                    P.op("vector", lambda e, gp=gp, ysl=ysl: e.tensor_tensor(
                        out=ysl, in0=ysl, in1=dxt[:, gp, :, :], op=ALU.add))
                    hsl = Hb[:, g * 512:(g + 1) * 512].rearrange("p (h q) -> p h q", h=8)
                    th1 = P.op("vector", lambda e, g=g, hsl=hsl: e.tensor_tensor(
                        out=hsl, in0=hsl,
                        in1=gby[:, g * 8:(g + 1) * 8].unsqueeze(2).broadcast_to([128, 8, 64]),
                        op=ALU.mult), sV)
                    P.wait("vector", sV, th1)
                    th2 = P.op("vector", lambda e, gp=gp, hsl=hsl: e.tensor_tensor(
                        out=hsl, in0=hsl,
                        in1=bankB(gp).rearrange("p (h q) -> p h q", h=8), op=ALU.add), sV)
                    P.wait("vector", sV, th2)
                    t_group_v[g] = P.op("vector", lambda e, g=g: e.tensor_copy(
                        out=Hbf[:, g * 512:(g + 1) * 512],
                        in_=Hb[:, g * 512:(g + 1) * 512]), sV)
                    bank_free["A%d" % gp] = (sV, t_group_v[g])
                    bank_free["A%d" % (2 + gp)] = (sV, t_group_v[g])
                    bank_free["B%d" % gp] = (sV, t_group_v[g])
                bank_free["XT%d" % cp] = (sV, t_group_v[NG - 1])
                t_arB_v = t_group_v[NG - 1]
                pc.update(t_cbt=t_cbt, tgby=tgby, gmm_last=t_gmm)

                P.wait("vector", sV, t_group_v[NG - 1])
                tsq0 = P.op("vector", lambda e: e.scalar_tensor_tensor(
                    out=sqdump, in0=Ybuf, scalar=1.0, in1=Ybuf,
                    op0=ALU.mult, op1=ALU.mult, accum_out=ssq), sV)
                P.wait("vector", sV, tsq0)
                t_sq = P.op("vector", lambda e: e.tensor_scalar(
                    out=lnvar, in0=ssq, scalar1=1.0 / INTER, scalar2=EPS,
                    op0=ALU.mult, op1=ALU.add), sV)
                P.wait("scalar", sV, t_sq)
                tgl0 = P.op("scalar", lambda e: e.activation(out=lnln, in_=lnvar, func=AF.Ln), sS)
                P.wait("scalar", sS, tgl0)
                trst = P.op("scalar", lambda e: e.activation(
                    out=grstd, in_=lnln, func=AF.Exp, scale=-0.5), sS)
                P.wait("vector", sS, trst)
                tys = P.op("vector", lambda e: e.tensor_scalar_mul(
                    out=ysb, in0=Ybuf, scalar1=grstd), sV)
                t_gating_v = tys
                t_arB_v = tys
                P.wait("tensor", sV, tys)
                if c == 0:
                    P.wait("scalar", sT, t_hbuf_cons)
                t_ycp = 0
                for r in range(4):
                    srcs = [ysb[:, (r * 8 + j) * 128:(r * 8 + j + 1) * 128]
                            for j in range(8)]
                    pre_cp = []
                    if r == 0:
                        pre_cp.append(("scalar", sGo, pc.get("t_ybt_dma", 0)))
                    ltr, t_ycp = tr_batch(srcs, "scalar", sS,
                                          dst2d=ybtstg[:, r * 1024:(r + 1) * 1024],
                                          pre_cp=pre_cp)
                pc["ytr"] = ltr
                P.wait("gpsimd", sS, t_ycp)
                if c == 0:
                    P.wait("gpsimd", sWt, t_yb_reads)
                pc["t_ybt_dma"] = P.op("gpsimd", lambda e, c=c: e.dma_start(
                    out=ybdT[:, :, c, :].transpose([1, 0, 2]),
                    in_=ybtstg.rearrange("p (j n) -> p j n", j=32)), sGo, 16)
            t_yb_done = sGo.val

            if last_step and PH < 3:
                continue
            t_ops_w = 0
            t_glv_prev_last = 0
            t_mmfin_prev = 0
            for dh in range(2):
                t_glcons = {}
                t_glv = {}
                t_mmfin = 0
                for j in range(32):
                    jp = j % 4
                    if j % 2 == 0:
                        if j == 0 and dh == 0:
                            P.wait("sync", sGo, t_yb_done)
                            if first:
                                P.wait("sync", sCo, t_wout_cast)
                        if j < 4 and dh == 1:
                            P.wait("sync", sV, t_glv_prev_last)
                            P.wait("sync", sT, t_mmfin_prev)
                        if j >= 4:
                            P.wait("sync", sV, t_glv[j - 3])
                            P.wait("sync", sT, t_glcons[j - 3])
                        tg1 = P.op("sync", lambda e, j=j, jp=jp: e.dma_start(
                            out=glraw[:, jp:jp + 2, :],
                            in_=ybdT[j:j + 2].rearrange("j p c t -> p j (c t)")), sWt, 16)
                        P.op("sync", lambda e, j=j, jp=jp: e.dma_start(
                            out=szw[:, jp:jp + 2, :],
                            in_=szd[j * 128:(j + 2) * 128, :]
                            .rearrange("(j p) t -> p j t", p=128)), sWw, 16)
                        tg3 = P.op("sync", lambda e, j=j, jp=jp, dh=dh: e.dma_start(
                            out=wow4v[:, jp:jp + 2, :],
                            in_=woutb[j * 128:(j + 2) * 128, dh * 1024:(dh + 1) * 1024]
                            .rearrange("(j p) n -> p j n", p=128)), sWw, 16)
                        tg1_pair, tg3_pair = tg1, tg3
                    P.wait("vector", sWt, tg1_pair)
                    P.wait("vector", sWw, tg3_pair)
                    if j >= 4:
                        P.wait("vector", sT, t_glcons[j - 4])
                    elif dh == 1:
                        P.wait("vector", sT, t_mmfin_prev)
                    t_glv[j] = P.op("vector", lambda e, j=j, jp=jp: e.scalar_tensor_tensor(
                        out=glraw[:, jp, :], in0=glraw[:, jp, :],
                        scalar=c_normw[:, j:j + 1], in1=szw[:, jp, :],
                        op0=ALU.mult, op1=ALU.mult), sV)
                    P.wait("tensor", sV, t_glv[j])
                    if j == 0:
                        wait_bank("tensor", *["A%d" % b for b in range(4)],
                                  *["B%d" % b for b in range(4)])
                    for tti in range(4):
                        for dq in range(2):
                            bk = bankA(tti) if dq == 0 else bankB(tti)
                            fn = (lambda e, j=j, jp=jp, tti=tti, dq=dq, bk=bk: e.matmul(
                                out=bk, lhsT=glraw[:, jp, tti * 128:(tti + 1) * 128],
                                rhs=wow4v[:, jp, dq * 512:(dq + 1) * 512],
                                start=(j == 0), stop=(j == 31)))
                            if tti == 3 and dq == 1:
                                t_glcons[j] = P.op("tensor", fn, sT)
                                if j == 31:
                                    t_mmfin = t_glcons[j]
                            else:
                                P.op("tensor", fn)
                t_glv_prev_last = t_glv[31]
                t_mmfin_prev = t_mmfin
                t_yb_reads = sWt.val
                t_sz_reads = sWw.val
                for tti in range(4):
                    for dq in range(2):
                        bk = bankA(tti) if dq == 0 else bankB(tti)
                        P.wait("vector", sT, t_mmfin)
                        if dh == 0 and tti == 0 and dq == 0:
                            P.wait("vector", sGo, t_yb_done)
                        te = P.op("vector", lambda e, bk=bk, tti=tti, dh=dh, dq=dq:
                                  e.tensor_copy(
                                      out=aosb[:, tti * 2048 + dh * 1024 + dq * 512:
                                               tti * 2048 + dh * 1024 + (dq + 1) * 512],
                                      in_=bk), sV)
                        bank_free[("A%d" if dq == 0 else "B%d") % tti] = (sV, te)
            t_x1w = 0
            for tt in range(4):
                P.wait("gpsimd", sV, t_arB_v)
                if tt == 0:
                    P.wait("gpsimd", sGa, t_x1_reads)
                tx = P.op("gpsimd", lambda e, s=s, r0=t0 + tt * 128: e.dma_start(
                    out=xtileB, in_=x_in[s, r0:r0 + 128, :]), sGa, 16)
                if t_x1w:
                    P.wait("vector", sGo, t_x1w)
                trs = ln_stats(aosb[:, tt * 2048:(tt + 1) * 2048], HID)
                P.wait("vector", sS, trs)
                P.wait("vector", sGa, tx)
                tn1 = P.op("vector", lambda e, tt=tt: e.tensor_scalar(
                    out=aosb[:, tt * 2048:(tt + 1) * 2048],
                    in0=aosb[:, tt * 2048:(tt + 1) * 2048],
                    scalar1=lnm, scalar2=lnrstd,
                    op0=ALU.subtract, op1=ALU.mult), sV)
                P.wait("vector", sV, tn1)
                tx1 = P.op("vector", lambda e, tt=tt: e.tensor_add(
                    out=aosb[:, tt * 2048:(tt + 1) * 2048],
                    in0=aosb[:, tt * 2048:(tt + 1) * 2048], in1=xtileB), sV)
                t_arB_v = tx1
                P.wait("gpsimd", sV, tx1)
                t_x1w = P.op("gpsimd", lambda e, tt=tt: e.dma_start(
                    out=x1b[tt * 128:(tt + 1) * 128, :],
                    in_=aosb[:, tt * 2048:(tt + 1) * 2048]), sGo, 16)

            if last_step and PH < 4:
                continue
            tr_hist = {}
            for tt in range(4):
                if tt >= 2:
                    P.wait("vector", sT, tr_hist[tt - 2])
                if tt == 0:
                    P.wait("vector", sGo, t_x1w)
                trs = ln_stats(aosb[:, tt * 2048:(tt + 1) * 2048], HID)
                P.wait("vector", sS, trs)
                tn = P.op("vector", lambda e, tt=tt: e.tensor_scalar(
                    out=xnst[:, tt % 2, :], in0=aosb[:, tt * 2048:(tt + 1) * 2048],
                    scalar1=lnm, scalar2=lnrstd,
                    op0=ALU.subtract, op1=ALU.mult), sV)
                t_arB_v = tn
                P.wait("tensor", sV, tn)
                for r in range(2):
                    srcs = [xnst[:, tt % 2, (r * 8 + k) * 128:(r * 8 + k + 1) * 128]
                            for k in range(8)]
                    pre_cp = []
                    if tt == 0 and r == 0:
                        pre_cp.append(("vector", sT, t_colT_consumed))
                    ltr, t_colT_ready = tr_batch(
                        srcs, "vector", sV,
                        dst3d=colT[:, r * 8:(r + 1) * 8, tt * 128:(tt + 1) * 128],
                        pre_cp=pre_cp)
                tr_hist[tt] = ltr
            t_h = {}
            t_gu = {}
            cap_tr = {}
            cap_wo = {}
            for i in range(64):
                ip = i % 2
                if not first:
                    if i == 0:
                        P.wait("sync", sT, t_mmfin)
                        P.wait("sync", sGo, t_opsw_last)
                        if s == 0 and sc == 1:
                            P.wait("sync", sGo, t_cap_gu_done)
                    if i >= 2:
                        P.wait("sync", sT, t_gu[i - 2])
                    tga = P.op("sync", lambda e, i=i, ip=ip: e.dma_start(
                        out=wguw[ip][0], in_=wgutc[i]), sWw, 16)
                    tgb = P.op("sync", lambda e, i=i, ip=ip: e.dma_start(
                        out=wguw[ip][1], in_=wgutc[i + 64]), sWw, 16)
                    P.wait("tensor", sWw, tgb)
                else:
                    for half in (0, 1):
                        it = i + 64 * half
                        if i >= 1:
                            P.wait("sync", sT, cap_tr[(i - 1, half)])
                        if i == 0 and half == 0:
                            P.wait("sync", sCb, t_wgud_cast)
                        tld = P.op("sync", lambda e, it=it, half=half: e.dma_start(
                            out=stag[half],
                            in_=wgub[it * 128:(it + 1) * 128, :]
                            .rearrange("p (j n) -> p j n", j=16)), sWw, 16)
                        P.wait("tensor", sWw, tld)
                        t_ccp = 0
                        for r in range(2):
                            srcs = [stag[half][:, r * 8 + j, :] for j in range(8)]
                            pre_cp = []
                            if r == 0:
                                if i >= 2:
                                    pre_cp += [("vector", sT, t_gu[i - 2]),
                                               ("vector", sGo, cap_wo[(i - 2, half)])]
                                else:
                                    pre_cp += [("vector", sT, t_mmfin),
                                               ("vector", sGo, t_opsw_last)]
                            ltr, t_ccp = tr_batch(
                                srcs, "vector", sV,
                                dst2d=wguwF[ip][half][:, r * 1024:(r + 1) * 1024],
                                pre_cp=pre_cp)
                        cap_tr[(i, half)] = ltr
                        P.wait("gpsimd", sV, t_ccp)
                        cap_wo[(i, half)] = P.op("gpsimd", lambda e, it=it, ip=ip, half=half:
                                                 e.dma_start(out=wgutc[it],
                                                             in_=wguw[ip][half]), sGo, 16)
                    t_cap_gu_done = cap_wo[(63, 1)] if i == 63 else t_cap_gu_done
                    P.wait("tensor", sV, t_ccp)
                if i == 0:
                    P.wait("tensor", sV, t_colT_ready)
                wait_bank("tensor", "A%d" % ip, "A%d" % (2 + ip))
                for k in range(16):
                    P.op("tensor", lambda e, ip=ip, k=k: e.matmul(
                        out=bankA(ip), lhsT=wguw[ip][0][:, k, :], rhs=colT[:, k, :],
                        start=(k == 0), stop=(k == 15)))
                for k in range(16):
                    fn = (lambda e, ip=ip, k=k: e.matmul(
                        out=bankA(2 + ip), lhsT=wguw[ip][1][:, k, :], rhs=colT[:, k, :],
                        start=(k == 0), stop=(k == 15)))
                    if k == 15:
                        t_gu[i] = P.op("tensor", fn, sT)
                    else:
                        P.op("tensor", fn)
                P.wait("scalar", sT, t_gu[i])
                if i >= 2:
                    P.wait("scalar", sV, t_h[i - 2])
                tsg = P.op("scalar", lambda e, ip=ip: e.activation(
                    out=sgst[:, ip, :], in_=bankA(ip), func=AF.Silu), sS)
                bank_free["A%d" % ip] = (sS, tsg)
                P.wait("vector", sS, tsg)
                if i == 0:
                    P.wait("vector", sT, t_hbuf_cons)
                t_h[i] = P.op("vector", lambda e, i=i, ip=ip: e.tensor_tensor(
                    out=hbuf[:, i, :], in0=sgst[:, ip, :], in1=bankA(2 + ip),
                    op=ALU.mult), sV)
                bank_free["A%d" % (2 + ip)] = (sV, t_h[i])
            t_colT_consumed = t_gu[63]
            t_dn = {}
            t_dn_half = {}
            t_ff_w = 0
            cap_dtr = {}
            cap_dwo = {}
            for d in range(16):
                dp = d % 2
                for qr in range(4):
                    hb = (4 * d + qr) % 2
                    q_idx = 4 * d + qr
                    if not first:
                        if q_idx == 0:
                            P.wait("sync", sT, t_hbuf_cons)
                            if s == 0 and sc == 1:
                                P.wait("sync", sGo, t_cap_dn_done)
                        if q_idx >= 2:
                            P.wait("sync", sT, t_dn_half[q_idx - 2])
                        twd = P.op("sync", lambda e, d=d, qr=qr, hb=hb: e.dma_start(
                            out=wdw[hb], in_=wdtc[d, :, qr * 16:(qr + 1) * 16, :]), sWw, 16)
                        P.wait("tensor", sWw, twd)
                    else:
                        if q_idx >= 2:
                            P.wait("sync", sT, cap_dtr[q_idx - 2])
                        else:
                            P.wait("sync", sT, cap_tr[(63, hb)])
                        tld = P.op("sync", lambda e, d=d, qr=qr: e.dma_start(
                            out=stag[(4 * d + qr) % 2],
                            in_=wdb[d * 128:(d + 1) * 128, qr * 2048:(qr + 1) * 2048]
                            .rearrange("p (j n) -> p j n", j=16)), sWw, 16)
                        P.wait("tensor", sWw, tld)
                        t_ccp = 0
                        for r in range(2):
                            srcs = [stag[hb][:, r * 8 + j, :] for j in range(8)]
                            pre_cp = []
                            if r == 0:
                                if q_idx >= 2:
                                    pre_cp += [("vector", sT, t_dn_half[q_idx - 2]),
                                               ("vector", sGo, cap_dwo[q_idx - 2])]
                                else:
                                    pre_cp += [("vector", sT, t_hbuf_cons)]
                            ltr, t_ccp = tr_batch(
                                srcs, "vector", sV,
                                dst2d=wdwF[hb][:, r * 1024:(r + 1) * 1024],
                                pre_cp=pre_cp)
                        cap_dtr[q_idx] = ltr
                        P.wait("gpsimd", sV, t_ccp)
                        cap_dwo[q_idx] = P.op("gpsimd", lambda e, d=d, qr=qr, hb=hb:
                                              e.dma_start(out=wdtc[d, :, qr * 16:(qr + 1) * 16, :],
                                                          in_=wdw[hb]), sGo, 16)
                        if q_idx == 63:
                            t_cap_dn_done = cap_dwo[q_idx]
                        P.wait("tensor", sV, t_ccp)
                    if d == 0 and qr == 0:
                        P.wait("tensor", sV, t_h[63])
                        wait_bank("tensor", "B0", "B1")
                    if qr == 0:
                        wait_bank("tensor", "B%d" % dp)
                    for ii in range(16):
                        iglob = qr * 16 + ii
                        fn = (lambda e, d=d, dp=dp, hb=hb, ii=ii, iglob=iglob: e.matmul(
                            out=bankB(dp), lhsT=wdw[hb][:, ii, :], rhs=hbuf[:, iglob, :],
                            start=(iglob == 0), stop=(iglob == 63)))
                        if ii == 15:
                            tlast = P.op("tensor", fn, sT)
                            t_dn_half[4 * d + qr] = tlast
                            if qr == 3:
                                t_dn[d] = tlast
                        else:
                            P.op("tensor", fn)
                P.wait("vector", sT, t_dn[d])
                P.wait("vector", sGo, t_ff_w)
                tfe = P.op("vector", lambda e, d=d, dp=dp: e.tensor_copy(
                    out=ffst[:, dp, :], in_=bankB(dp)), sV)
                bank_free["B%d" % dp] = (sV, tfe)
                P.wait("gpsimd", sV, tfe)
                t_ff_w = P.op("gpsimd", lambda e, d=d, dp=dp: e.dma_start(
                    out=ffbd[d * 128:(d + 1) * 128, :], in_=ffst[:, dp, :]), sGo, 16)
            t_hbuf_cons = t_dn[15]
            t_ln4v = {}
            for tt in range(4):
                ftp = tt % 2
                P.wait("sync", sGo, t_ff_w)
                if tt >= 2:
                    P.wait("sync", sV, t_ln4v[tt - 2])
                t_fft = P.op("sync", lambda e, tt=tt, ftp=ftp: e.dma_start(
                    out=fftl[ftp], in_=ffbd[:, tt * 128:(tt + 1) * 128], transpose=True), sWt, 16)
                P.wait("gpsimd", sV, t_arB_v)
                tx = P.op("gpsimd", lambda e, tt=tt: e.dma_start(
                    out=xtileA, in_=x1b[tt * 128:(tt + 1) * 128, :]), sGa, 16)
                P.wait("vector", sWt, t_fft)
                if t_yout_last:
                    P.wait("vector", sGo, t_yout_last)
                trs = ln_stats(fftl[ftp], HID)
                P.wait("vector", sS, trs)
                P.wait("vector", sGa, tx)
                tn1 = P.op("vector", lambda e, ftp=ftp: e.tensor_scalar(
                    out=outstg[:, 0:2048], in0=fftl[ftp], scalar1=lnm, scalar2=lnrstd,
                    op0=ALU.subtract, op1=ALU.mult), sV)
                P.wait("vector", sV, tn1)
                t_ln4v[tt] = P.op("vector", lambda e: e.tensor_add(
                    out=outstg[:, 0:2048], in0=outstg[:, 0:2048], in1=xtileA), sV)
                t_arB_v = t_ln4v[tt]
                P.wait("gpsimd", sV, t_ln4v[tt])
                t_yout_last = P.op("gpsimd", lambda e, s=s, r0=t0 + tt * 128: e.dma_start(
                    out=y_out[s, r0:r0 + 128, :], in_=outstg[:, 0:2048]), sGo, 16)
            t_x1_reads = sGa.val

    P.wait("gpsimd", sGo, sGo.val)
    P.wait("gpsimd", sGa, sGa.val)
    P.wait("gpsimd", sGr, sGr.val)
    P.wait("gpsimd", sGs, sGs.val)
    P.wait("gpsimd", sCa, sCa.val)
    P.wait("gpsimd", sCo, sCo.val)
    P.wait("gpsimd", sCb, sCb.val)
    P.wait("sync", sWw, sWw.val)
    P.wait("sync", sWt, sWt.val)

    blk = ctx.enter_context(nc.Block())
    for eng in ("sync", "gpsimd", "scalar", "vector", "tensor"):
        def runner(e, ops=tuple(P.q[eng])):
            for f in ops:
                f(e)
        getattr(blk, eng)(runner)
    ctx.close()
    return nc


_NC_CACHE = None


def kernel(**inputs):
    global _NC_CACHE
    x = np.ascontiguousarray(inputs["L0_attn_hidden_states"])
    consts = dict(
        idf_i=np.eye(128, dtype=np.float32),
        idb_i=np.eye(128, dtype=ml_dtypes.bfloat16),
        umask_i=np.triu(np.ones((128, 128), np.float32)),
    )
    weights = dict(
        w1=np.ascontiguousarray(inputs["L0_attn_in_proj_w"], dtype=np.float32),
        convw_i=np.ascontiguousarray(inputs["L0_attn_conv_w"], dtype=np.float32),
        convb_i=np.ascontiguousarray(inputs["L0_attn_conv_b"], dtype=np.float32).reshape(34, 128),
        alog_i=np.ascontiguousarray(inputs["L0_attn_A_log"], dtype=np.float32).reshape(1, 64),
        dvec_i=np.ascontiguousarray(inputs["L0_attn_D"], dtype=np.float32).reshape(1, 64),
        dtb_i=np.ascontiguousarray(inputs["L0_attn_dt_bias"], dtype=np.float32).reshape(1, 64),
        normw_i=np.ascontiguousarray(inputs["L0_attn_norm_w"], dtype=np.float32).reshape(32, 128),
        wout=np.ascontiguousarray(inputs["L0_attn_out_proj_w"], dtype=np.float32),
        wgu=np.ascontiguousarray(inputs["L0_ffn_w_gate_up"], dtype=np.float32),
        wdn=np.ascontiguousarray(inputs["L0_ffn_w_down"], dtype=np.float32),
    )
    if _NC_CACHE is None:
        _NC_CACHE = build_nc()
    nc = _NC_CACHE
    in_maps = []
    for core in range(NCORES):
        m = dict(consts)
        m.update(weights)
        m["x_in"] = np.ascontiguousarray(x[2 * core:2 * core + 2])
        in_maps.append(m)
    res = run_bass_kernel_spmd(nc, in_maps, core_ids=list(range(NCORES)))
    out = np.empty((16, 2048, HID), np.float32)
    for core in range(NCORES):
        out[2 * core:2 * core + 2] = res.results[core]["y_out"]
    return out


# version 14
# speedup vs baseline: 1.0250x; 1.0250x over previous
import numpy as np
import ml_dtypes
import concourse.bass as bass
import concourse.mybir as mybir
from concourse.bass_utils import run_bass_kernel_spmd

F32 = mybir.dt.float32
BF16 = mybir.dt.bfloat16
AF = mybir.ActivationFunctionType
ALU = mybir.AluOpType
AX = mybir.AxisListType

NCORES = 8
HID = 2048
PROJ = 8512
INTER = 4096
NFT = 67
HEADS = 64
Q = 128
SC = 512
NCH = SC // Q
NSC = 2048 // SC
NSEQ = 2
NG = 8
FFI = 8192
EPS = 1e-5


class Sem:
    def __init__(self, ctx, nc, name):
        self.h = ctx.enter_context(nc.semaphore(name))
        self.val = 0

    def tick(self, k=1):
        self.val += k
        return self.val


class Prog:
    def __init__(self):
        self.q = {k: [] for k in ("sync", "gpsimd", "scalar", "vector", "tensor")}

    def wait(self, eng, sem, val):
        if val and val > 0:
            self.q[eng].append(lambda e, s=sem.h, v=val: e.wait_ge(s, v))

    def op(self, eng, fn, sem=None, k=1, serialize=True):
        if sem is None:
            self.q[eng].append(fn)
            return None
        if k == 16 and serialize:
            self.wait(eng, sem, sem.val)
        t = sem.tick(k)
        self.q[eng].append(lambda e, f=fn, s=sem.h, kk=k: f(e).then_inc(s, kk))
        return t


def build_nc(n_steps=None, phase_limit="E", debug=False):
    import contextlib
    nc = bass.Bass("TRN2", target_bir_lowering=False)
    ctx = contextlib.ExitStack()
    PH = "ABCDE".index(phase_limit)
    kind_dbg = "ExternalOutput" if debug else "Internal"


    x_in = nc.dram_tensor("x_in", [NSEQ, 2048, HID], F32, kind="ExternalInput").ap()
    w1 = nc.dram_tensor("w1", [HID, PROJ], F32, kind="ExternalInput").ap()
    convw_i = nc.dram_tensor("convw_i", [4352, 4], F32, kind="ExternalInput").ap()
    convb_i = nc.dram_tensor("convb_i", [34, 128], F32, kind="ExternalInput").ap()
    alog_i = nc.dram_tensor("alog_i", [1, 64], F32, kind="ExternalInput").ap()
    dvec_i = nc.dram_tensor("dvec_i", [1, 64], F32, kind="ExternalInput").ap()
    dtb_i = nc.dram_tensor("dtb_i", [1, 64], F32, kind="ExternalInput").ap()
    normw_i = nc.dram_tensor("normw_i", [32, 128], F32, kind="ExternalInput").ap()
    wout = nc.dram_tensor("wout", [INTER, HID], F32, kind="ExternalInput").ap()
    wgu = nc.dram_tensor("wgu", [2 * FFI, HID], F32, kind="ExternalInput").ap()
    wdn = nc.dram_tensor("wdn", [HID, FFI], F32, kind="ExternalInput").ap()
    idf_i = nc.dram_tensor("idf_i", [128, 128], F32, kind="ExternalInput").ap()
    idb_i = nc.dram_tensor("idb_i", [128, 128], BF16, kind="ExternalInput").ap()
    umask_i = nc.dram_tensor("umask_i", [128, 128], F32, kind="ExternalInput").ap()

    y_out = nc.dram_tensor("y_out", [NSEQ, 2048, HID], F32, kind="ExternalOutput").ap()

    w1b = nc.dram_tensor("w1b", [HID, PROJ], BF16).ap()
    woutb = nc.dram_tensor("woutb", [INTER, HID], BF16).ap()
    wgub = nc.dram_tensor("wgub", [2 * FFI, HID], BF16).ap()
    wgutc = nc.dram_tensor("wgutc", [128, 128, 16, 128], BF16).ap()
    wdb = nc.dram_tensor("wdb", [HID, FFI], BF16).ap()
    wdtc = nc.dram_tensor("wdtc", [16, 128, 64, 128], BF16).ap()
    xbd = nc.dram_tensor("xbd", [INTER, SC], BF16, kind=kind_dbg).ap()
    szd = nc.dram_tensor("szd", [INTER, SC], BF16, kind=kind_dbg).ap()
    ybdT = nc.dram_tensor("ybdT", [32, 128, NCH, 128], BF16).ap()
    x1b = nc.dram_tensor("x1b", [SC, HID], F32, kind=kind_dbg).ap()
    ffbdT = nc.dram_tensor("ffbdT", [4, 128, 16, 128], BF16).ap()
    gsc3 = nc.dram_tensor("gsc3", [NCH, 8192], F32).ap()
    gamdF = nc.dram_tensor("gamdF", [NCH, 64], F32).ap()

    def T(name, shape, dt=F32):
        return ctx.enter_context(nc.sbuf_tensor(name, shape, dt))[:]

    c_idf = T("c_idf", [128, 128])
    c_idb = T("c_idb", [128, 128], BF16)
    c_umask = T("c_umask", [128, 128])
    c_convw = T("c_convw", [128, 34, 4])
    c_convb = T("c_convb", [128, 34])
    c_normw = T("c_normw", [128, 32])
    c_dbc = T("c_dbc", [128, 64])
    c_ones64 = T("c_ones64", [64, 1])
    c_acol = T("c_acol", [64, 1])
    c_eacol = T("c_eacol", [64, 1])
    c_dtb = T("c_dtb", [64, 1])
    st_rows = T("st_rows", [34, 128])
    s64 = T("s64", [64, 4])

    Hb = T("Hb", [128, INTER])
    Hbf = T("Hbf", [128, INTER], BF16)
    convh = T("convh", [128, NSEQ, 34, 3])
    colT = T("colT", [128, 16, SC], BF16)
    Bb = T("Bb", [128, SC], BF16)
    Cb = T("Cb", [128, SC], BF16)
    BTb = T("BTb", [128, 128], BF16)
    dtF = T("dtF", [64, SC])

    arA = T("arA", [128, 18432])
    arB = T("arB", [128, 7168])
    arC = T("arC", [128, 6144])

    prepP1 = T("prepP1", [128, 512])
    prepP2 = T("prepP2", [128, 512])
    GinF = prepP1[0:64, :]
    cumF = prepP1[64:128, :]
    dEndArgF = prepP2[0:64, :]
    GT = T("GT", [128, 64])
    dInT = T("dInT", [128, 64])
    dEndT = T("dEndT", [128, 64])
    dtT = T("dtT", [128, 64])
    gby = T("gby", [128, 64])
    CBm = T("CBm", [128, 128])
    mtb = T("mtb", [128, 2, 8, 128], BF16)
    xdt = T("xdt", [128, 2, 8, 64], BF16)
    xwv = T("xwv", [128, 2, 8, 64], BF16)
    dxt = T("dxt", [128, 2, 8, 64])

    xnst = T("xnst", [128, 2, HID], BF16)
    xsil = T("xsil", [128, 2, SC], BF16)
    glraw = T("glraw", [128, 4, SC], BF16)
    szw = T("szw", [128, 4, SC], BF16)
    smalls = T("smalls", [128, 16])

    psA = ctx.enter_context(nc.psum_tensor("psA", [128, 2048], F32))[:]
    psB = ctx.enter_context(nc.psum_tensor("psB", [128, 2048], F32))[:]

    hbuf = arA[:, 0:16384].bitcast(BF16).rearrange("p (i t) -> p i t", i=64)
    Ybuf = arA[:, 0:4096]
    XTc = arA[:, 4096:8192].bitcast(BF16).rearrange("p (c t) -> p c t", c=2)
    stag = [arA[:, 16384:17408].bitcast(BF16).rearrange("p (j n) -> p j n", j=16),
            arA[:, 17408:18432].bitcast(BF16).rearrange("p (j n) -> p j n", j=16)]
    ybtstg = arA[:, 8192:10240].bitcast(BF16)
    aosb = arA[:, 8192:16384]
    sgst = arB[:, 6144:7168].rearrange("p (a t) -> p a t", a=2)
    cacc = arB[:, 2048:3072].rearrange("p (a t) -> p a t", a=2)
    grpv = [arB[:, 0:1024], arB[:, 1024:2048]]
    dbufv = arB[:, 2048:3072]
    dminv = [arB[:, 3072:4096], arB[:, 4096:5120]]
    expdv = [arB[:, 5120:6144], arB[:, 6144:7168]]
    sqdump = arB[:, 0:4096]
    ysb = arB[:, 4096:6144].bitcast(BF16)
    xtileA = arB[:, 0:2048]
    xtileB = arB[:, 2048:4096]
    lndump = arB[:, 4096:6144]
    outstg = arB[:, 4096:6144]
    dt_t1 = arB[:, 0:512][0:64, :]
    dt_t2 = arB[:, 512:1024][0:64, :]
    dt_t3 = arB[:, 1024:1536][0:64, :]

    w1w = [arC[:, 0:1024].bitcast(BF16).rearrange("p (k n) -> p k n", k=16),
           arC[:, 1024:2048].bitcast(BF16).rearrange("p (k n) -> p k n", k=16)]
    wow4v = arC[:, 2048:4096].bitcast(BF16).rearrange("p (j n) -> p j n", j=4)
    stagF = [arC[:, 2048:2304].bitcast(BF16), arC[:, 2304:2560].bitcast(BF16)]
    wguw = [[arC[:, 0:1024].bitcast(BF16).rearrange("p (k n) -> p k n", k=16),
             arC[:, 1024:2048].bitcast(BF16).rearrange("p (k n) -> p k n", k=16)],
            [arC[:, 2048:3072].bitcast(BF16).rearrange("p (k n) -> p k n", k=16),
             arC[:, 3072:4096].bitcast(BF16).rearrange("p (k n) -> p k n", k=16)]]
    wdw = [arC[:, 4096:5120].bitcast(BF16).rearrange("p (i n) -> p i n", i=16),
           arC[:, 5120:6144].bitcast(BF16).rearrange("p (i n) -> p i n", i=16)]
    wguwF = [[arC[:, 0:1024].bitcast(BF16), arC[:, 1024:2048].bitcast(BF16)],
             [arC[:, 2048:3072].bitcast(BF16), arC[:, 3072:4096].bitcast(BF16)]]
    wdwF = [arC[:, 4096:5120].bitcast(BF16), arC[:, 5120:6144].bitcast(BF16)]
    fftl = [arC[:, 0:1024].bitcast(BF16), arC[:, 1024:2048].bitcast(BF16)]
    xbcp = arB[:, 3072:4102].rearrange("p (a t) -> p a t", a=2)
    ffst = arB[:, 6144:7168].bitcast(BF16).rearrange("p (a t) -> p a t", a=2)[:, :, 0:512]

    lnsum = smalls[:, 0:1]
    lnm = smalls[:, 1:2]
    lnmsq = smalls[:, 2:3]
    lnvar = smalls[:, 3:4]
    lnln = smalls[:, 4:5]
    lnrstd = smalls[:, 5:6]
    ssq = smalls[:, 6:7]
    grstd = smalls[:, 7:8]

    def bankA(i):
        return psA[:, i * 512:(i + 1) * 512]

    def bankB(i):
        return psB[:, i * 512:(i + 1) * 512]

    ps_cbt = psB[:, 1024:1152]
    ps_gt = psB[:, 1152:1216]
    ps_dint = psB[:, 1216:1280]
    ps_dendt = psB[:, 1280:1344]
    ps_dtt = psB[:, 1344:1408]
    ps_bt = psB[:, 1536:1600].bitcast(BF16)
    psTB = [psB[:, 1024:1536].bitcast(BF16), psB[:, 1536:2048].bitcast(BF16)]

    P = Prog()
    sWw = Sem(ctx, nc, "sWw")
    sWt = Sem(ctx, nc, "sWt")
    sGa = Sem(ctx, nc, "sGa")
    sGo = Sem(ctx, nc, "sGo")
    sGr = Sem(ctx, nc, "sGr")
    sGs = Sem(ctx, nc, "sGs")
    sT = Sem(ctx, nc, "sT")
    sV = Sem(ctx, nc, "sV")
    sS = Sem(ctx, nc, "sS")
    sCa = Sem(ctx, nc, "sCa")
    sCo = Sem(ctx, nc, "sCo")
    sCb = Sem(ctx, nc, "sCb")

    bank_free = {}

    def wait_bank(eng, *keys):
        for key in keys:
            if key in bank_free:
                sem, val = bank_free.pop(key)
                P.wait(eng, sem, val)

    tb_rot = [0]

    def tr_batch(srcs, copy_eng, copy_sem, dst2d=None, dst3d=None, pre_cp=()):
        bank = tb_rot[0]
        tb_rot[0] ^= 1
        wait_bank("tensor", "B%d" % (2 + bank))
        last_tr = None
        for idx, s0 in enumerate(srcs):
            fn = (lambda e, s0=s0, idx=idx, bank=bank: e.transpose(
                out=psTB[bank][:, idx * 128:(idx + 1) * 128], in_=s0, identity=c_idb))
            last_tr = P.op("tensor", fn, sT) if idx == len(srcs) - 1 else P.op("tensor", fn)
        for (weng, wsem, wval) in pre_cp:
            P.wait(copy_eng, wsem, wval)
        P.wait(copy_eng, sT, last_tr)
        n = len(srcs)
        if copy_eng == "scalar":
            tcp = P.op("scalar", lambda e, bank=bank, n=n, dst2d=dst2d: e.activation(
                out=dst2d, in_=psTB[bank][:, 0:n * 128], func=AF.Copy), copy_sem)
        elif dst2d is not None:
            tcp = P.op("vector", lambda e, bank=bank, n=n, dst2d=dst2d: e.tensor_copy(
                out=dst2d, in_=psTB[bank][:, 0:n * 128]), copy_sem)
        else:
            src_ap = psTB[bank][:, 0:n * 128].rearrange("p (j n) -> p j n", j=n)
            tcp = P.op("vector", lambda e, dst3d=dst3d, src_ap=src_ap: e.tensor_copy(
                out=dst3d, in_=src_ap), copy_sem)
        bank_free["B%d" % (2 + bank)] = (copy_sem, tcp)
        return last_tr, tcp

    for dst, src in ((c_idf, idf_i), (c_idb, idb_i), (c_umask, umask_i)):
        P.op("gpsimd", lambda e, d=dst, s=src: e.dma_start(out=d, in_=s), sGs, 16)
    P.op("gpsimd", lambda e: e.dma_start(
        out=c_convw, in_=convw_i.rearrange("(j p) k -> p j k", p=128)), sGs, 16)
    P.op("gpsimd", lambda e: e.dma_start(out=st_rows, in_=convb_i), sGs, 16)
    t_cb_rows = sGs.val
    P.op("gpsimd", lambda e: e.dma_start(
        out=c_dbc, in_=dvec_i.partition_broadcast(128).squeeze(1)), sGs, 16)
    P.op("gpsimd", lambda e: e.dma_start(out=s64[0:64, 0:1], in_=alog_i.transpose([1, 0])), sGs, 16)
    P.op("gpsimd", lambda e: e.dma_start(out=s64[0:64, 1:2], in_=dtb_i.transpose([1, 0])), sGs, 16)
    t_small_rows = sGs.val
    P.wait("tensor", sGs, t_small_rows)
    t1 = P.op("tensor", lambda e: e.transpose(
        out=psB[:, 0:34], in_=st_rows, identity=c_idf[0:34, 0:34]), sT)
    P.wait("vector", sT, t1)
    t_cb_done = P.op("vector", lambda e: e.tensor_copy(out=c_convb, in_=psB[:, 0:34]), sV)
    P.wait("gpsimd", sV, t_cb_done)
    P.op("gpsimd", lambda e: e.dma_start(out=st_rows[0:32, :], in_=normw_i), sGs, 16)
    t_nw_rows = sGs.val
    P.wait("tensor", sGs, t_nw_rows)
    t2 = P.op("tensor", lambda e: e.transpose(
        out=psB[:, 64:96], in_=st_rows[0:32, :], identity=c_idf[0:32, 0:32]), sT)
    P.wait("vector", sT, t2)
    t_nw_done = P.op("vector", lambda e: e.tensor_copy(out=c_normw, in_=psB[:, 64:96]), sV)
    bank_free["B0"] = (sV, t_nw_done)
    P.wait("scalar", sGs, t_nw_rows)
    t3 = P.op("scalar", lambda e: e.activation(out=c_eacol, in_=s64[0:64, 0:1], func=AF.Exp), sS)
    P.wait("vector", sS, t3)
    P.op("vector", lambda e: e.tensor_scalar_mul(out=c_acol, in0=c_eacol, scalar1=-1.0))
    P.op("vector", lambda e: e.tensor_copy(out=c_dtb, in_=s64[0:64, 1:2]))
    P.op("vector", lambda e: e.memset(c_ones64, 1.0))
    P.op("vector", lambda e: e.memset(Hb, 0.0))
    P.op("vector", lambda e: e.memset(Hbf, 0.0))
    P.op("vector", lambda e: e.memset(convh, 0.0), sV)

    W1CH = [(0, 9), (9, 18), (18, 27), (27, 36), (36, 45), (45, 54), (54, 63), (63, 67)]

    def emit_w1_chunk(k):
        f0, f1 = W1CH[k]
        c0, c1 = f0 * 128, min(f1 * 128, PROJ)
        P.op("gpsimd", lambda e, c0=c0, c1=c1: e.dma_start(
            out=w1b[:, c0:c1], in_=w1[:, c0:c1]), sCa, 16)

    def emit_rest_casts():
        for i in range(4):
            P.op("gpsimd", lambda e, i=i: e.dma_start(
                out=woutb[i * 1024:(i + 1) * 1024, :], in_=wout[i * 1024:(i + 1) * 1024, :]),
                sCo, 16, serialize=False)
        for i in range(8):
            P.op("gpsimd", lambda e, i=i: e.dma_start(
                out=wgub[i * 2048:(i + 1) * 2048, :], in_=wgu[i * 2048:(i + 1) * 2048, :]),
                sCb, 16, serialize=False)
        for i in range(4):
            P.op("gpsimd", lambda e, i=i: e.dma_start(
                out=wdb[:, i * 2048:(i + 1) * 2048], in_=wdn[:, i * 2048:(i + 1) * 2048]),
                sCb, 16, serialize=False)

    t_w1_cast = 16 * len(W1CH)
    t_wout_cast = 4 * 16
    t_wgud_cast = 12 * 16

    sp_v0 = smalls[:, 12:13]
    sp_v1 = smalls[:, 13:14]
    sp_s0 = smalls[:, 14:15]
    sp_s1 = smalls[:, 15:16]

    def v_spacer():
        P.op("vector", lambda e: e.tensor_copy(out=sp_v0, in_=sp_v1))

    def s_spacer():
        P.op("scalar", lambda e: e.activation(out=sp_s0, in_=sp_s1, func=AF.Identity))

    def ln_stats(xtile, F):
        P.op("vector", lambda e, xtile=xtile: e.reduce_sum(out=lnsum, in_=xtile, axis=AX.X))
        t2 = P.op("vector", lambda e, xtile=xtile, F=F: e.scalar_tensor_tensor(
            out=lndump[:, 0:F], in0=xtile, scalar=1.0, in1=xtile,
            op0=ALU.mult, op1=ALU.mult, accum_out=ssq), sV)
        P.wait("vector", sV, t2)
        P.op("vector", lambda e, F=F: e.tensor_scalar_mul(out=lnm, in0=lnsum, scalar1=1.0 / F))
        t3 = P.op("vector", lambda e, F=F: e.tensor_scalar(
            out=lnvar, in0=ssq, scalar1=1.0 / F, scalar2=EPS, op0=ALU.mult, op1=ALU.add), sV)
        P.wait("vector", sV, t3)
        tm = P.op("vector", lambda e: e.tensor_mul(out=lnmsq, in0=lnm, in1=lnm), sV)
        P.wait("vector", sV, tm)
        t = P.op("vector", lambda e: e.tensor_sub(out=lnvar, in0=lnvar, in1=lnmsq), sV)
        P.wait("scalar", sV, t)
        ts5 = P.op("scalar", lambda e: e.activation(out=lnln, in_=lnvar, func=AF.Ln), sS)
        P.wait("scalar", sS, ts5)
        return P.op("scalar", lambda e: e.activation(
            out=lnrstd, in_=lnln, func=AF.Exp, scale=-0.5), sS)

    t_colT_consumed = 0
    t_arB_v = sV.val
    t_xt_reads = 0
    t_yb_reads = 0
    t_sz_reads = 0
    t_x1_reads = 0
    t_aob_reads = 0
    t_hbuf_cons = 0
    t_mmfin = 0
    t_opsw_last = 0
    t_gating_v = 0
    t_dt = 0
    t_yout_last = 0
    pc = {}
    t_cap_gu_done = 0
    t_cap_dn_done = 0

    step = 0
    stop_all = False
    for s in range(NSEQ):
        if stop_all:
            break
        if s > 0:
            P.wait("vector", sT, sT.val)
            P.op("vector", lambda e: e.memset(Hb, 0.0))
            tmr = P.op("vector", lambda e: e.memset(Hbf, 0.0), sV)
        for sc in range(NSC):
            if n_steps is not None and step >= n_steps:
                stop_all = True
                break
            step += 1
            last_step = (n_steps is not None and step == n_steps)
            t0 = sc * SC
            first = (s == 0 and sc == 0)
            tr_hist = {}
            for tt in range(4):
                xt = xtileA if tt % 2 == 0 else xtileB
                P.wait("gpsimd", sV, t_arB_v)
                if first:
                    emit_w1_chunk(2 * tt)
                tl = P.op("gpsimd", lambda e, s=s, r0=t0 + tt * 128, xt=xt: e.dma_start(
                    out=xt, in_=x_in[s, r0:r0 + 128, :]), sGa, 16)
                if first:
                    emit_w1_chunk(2 * tt + 1)
                    if tt == 3:
                        emit_rest_casts()
                P.wait("vector", sGa, tl)
                if tt >= 2:
                    P.wait("vector", sT, tr_hist[tt - 2])
                if tt == 0:
                    P.wait("vector", sGo, t_yout_last)
                trs = ln_stats(xt, HID)
                P.wait("vector", sS, trs)
                tn = P.op("vector", lambda e, xt=xt, tt=tt: e.tensor_scalar(
                    out=xnst[:, tt % 2, :], in0=xt, scalar1=lnm, scalar2=lnrstd,
                    op0=ALU.subtract, op1=ALU.mult), sV)
                t_arB_v = tn
                P.wait("tensor", sV, tn)
                for r in range(2):
                    srcs = [xnst[:, tt % 2, (r * 8 + k) * 128:(r * 8 + k + 1) * 128]
                            for k in range(8)]
                    pre_cp = []
                    if tt == 0 and r == 0:
                        pre_cp.append(("vector", sT, t_colT_consumed))
                    ltr, t_colT_ready = tr_batch(
                        srcs, "vector", sV,
                        dst3d=colT[:, r * 8:(r + 1) * 8, tt * 128:(tt + 1) * 128],
                        pre_cp=pre_cp)
                tr_hist[tt] = ltr

            if last_step and PH < 1:
                continue
            t_mm = {}
            t_xsil_dma = {0: 0, 1: 0}
            t_conv_v = {0: 0, 1: 0}
            P.wait("gpsimd", sWt, t_xt_reads)
            P.wait("gpsimd", sWw, t_sz_reads)
            for f in range(NFT):
                w = 128 if f < NFT - 1 else 64
                if f >= 2:
                    P.wait("sync", sT, t_mm[f - 2])
                if f == 0:
                    P.wait("sync", sGo, t_opsw_last)
                    P.wait("sync", sV, t_arB_v)
                if first:
                    P.wait("sync", sCa, 16 * min(f // 9 + 1, 8))
                tload = P.op("sync", lambda e, f=f, w=w: e.dma_start(
                    out=w1w[f % 2][:, :, 0:w],
                    in_=w1b[:, f * 128:f * 128 + w].rearrange("(k p) n -> p k n", p=128)), sWw, 16)
                P.wait("tensor", sWw, tload)
                if f == 0:
                    P.wait("tensor", sV, t_colT_ready)
                wait_bank("tensor", "A%d" % (f % 2))
                for k in range(16):
                    fn = (lambda e, f=f, k=k, w=w: e.matmul(
                        out=bankA(f % 2)[0:w, :], lhsT=w1w[f % 2][:, k, 0:w],
                        rhs=colT[:, k, :], start=(k == 0), stop=(k == 15)))
                    if k == 15:
                        t_mm[f] = P.op("tensor", fn, sT)
                    else:
                        P.op("tensor", fn)
                P.wait("scalar", sT, t_mm[f])
                if f < 32:
                    P.wait("scalar", sGo, t_xsil_dma[f % 2])
                    tz = P.op("scalar", lambda e, f=f: e.activation(
                        out=xsil[:, f % 2, :], in_=bankA(f % 2), func=AF.Silu, scale=0.25), sS)
                    bank_free["A%d" % (f % 2)] = (sS, tz)
                    P.wait("gpsimd", sS, tz)
                    t_xsil_dma[f % 2] = P.op("gpsimd", lambda e, f=f: e.dma_start(
                        out=szd[f * 128:(f + 1) * 128, :], in_=xsil[:, f % 2, :]), sGo, 16)
                elif f < 66:
                    ct = f - 32
                    P.wait("scalar", sV, t_conv_v[f % 2])
                    tc = P.op("scalar", lambda e, f=f: e.activation(
                        out=xbcp[:, f % 2, 3:515], in_=bankA(f % 2), func=AF.Copy, scale=0.5), sS)
                    bank_free["A%d" % (f % 2)] = (sS, tc)
                    P.wait("vector", sS, tc)
                    th0 = P.op("vector", lambda e, s=s, ct=ct, f=f: e.tensor_copy(
                        out=xbcp[:, f % 2, 0:3], in_=convh[:, s, ct, :]), sV)
                    P.wait("vector", sV, th0)
                    tcv0 = P.op("vector", lambda e, f=f, ct=ct: e.tensor_scalar_mul(
                        out=cacc[:, f % 2, :], in0=xbcp[:, f % 2, 0:512],
                        scalar1=c_convw[:, ct, 0:1]), sV)
                    for k in range(1, 4):
                        P.wait("vector", sV, tcv0)
                        tcv0 = P.op("vector", lambda e, f=f, ct=ct, k=k: e.scalar_tensor_tensor(
                            out=cacc[:, f % 2, :], in0=xbcp[:, f % 2, k:k + 512],
                            scalar=c_convw[:, ct, k:k + 1], in1=cacc[:, f % 2, :],
                            op0=ALU.mult, op1=ALU.add), sV)
                    P.wait("vector", sV, tcv0)
                    t_conv_v[f % 2] = P.op("vector", lambda e, s=s, ct=ct, f=f: e.tensor_copy(
                        out=convh[:, s, ct, :], in_=xbcp[:, f % 2, 512:515]), sV)
                    P.wait("scalar", sV, t_conv_v[f % 2])
                    if ct < 32:
                        P.wait("scalar", sGo, t_xsil_dma[f % 2])
                        tsl = P.op("scalar", lambda e, f=f, ct=ct: e.activation(
                            out=xsil[:, f % 2, :], in_=cacc[:, f % 2, :], func=AF.Silu,
                            bias=c_convb[:, ct:ct + 1]), sS)
                        P.wait("gpsimd", sS, tsl)
                        t_xsil_dma[f % 2] = P.op("gpsimd", lambda e, ct=ct, f=f: e.dma_start(
                            out=xbd[ct * 128:(ct + 1) * 128, :], in_=xsil[:, f % 2, :]), sGo, 16)
                    elif ct == 32:
                        P.wait("scalar", sT, pc.get("gmm_last", 0))
                        P.op("scalar", lambda e, f=f, ct=ct: e.activation(
                            out=Bb, in_=cacc[:, f % 2, :], func=AF.Silu,
                            bias=c_convb[:, ct:ct + 1]), sS)
                    else:
                        P.op("scalar", lambda e, f=f, ct=ct: e.activation(
                            out=Cb, in_=cacc[:, f % 2, :], func=AF.Silu,
                            bias=c_convb[:, ct:ct + 1]), sS)
                else:
                    P.wait("vector", sV, t_arB_v)
                    P.wait("vector", sT, t_mm[f])
                    td0 = P.op("vector", lambda e, f=f: e.tensor_scalar_add(
                        out=dt_t1, in0=bankA(f % 2)[0:64, :], scalar1=c_dtb), sV)
                    bank_free["A%d" % (f % 2)] = (sV, td0)
                    P.wait("vector", sV, td0)
                    tda = P.op("vector", lambda e: e.tensor_scalar_mul(
                        out=dt_t2, in0=dt_t1, scalar1=-1.0), sV)
                    P.wait("vector", sV, tda)
                    tv = P.op("vector", lambda e: e.tensor_tensor(
                        out=dt_t2, in0=dt_t1, in1=dt_t2, op=ALU.min), sV)
                    P.wait("scalar", sV, tv)
                    tse = P.op("scalar", lambda e: e.activation(out=dt_t3, in_=dt_t2, func=AF.Exp), sS)
                    P.wait("scalar", sS, tse)
                    ts2 = P.op("scalar", lambda e: e.activation(
                        out=dt_t3, in_=dt_t3, func=AF.Ln, bias=1.0), sS)
                    P.wait("vector", sS, ts2)
                    P.wait("vector", sV, tv)
                    tdb = P.op("vector", lambda e: e.tensor_scalar_max(
                        out=dt_t2, in0=dt_t1, scalar1=0.0), sV)
                    P.wait("vector", sV, tdb)
                    t_dt = P.op("vector", lambda e: e.tensor_add(out=dtF, in0=dt_t2, in1=dt_t3), sV)
                    t_arB_v = t_dt
            t_colT_consumed = t_mm[NFT - 1]
            t_BC_scalar = sS.val
            t_B_gp = sGo.val

            if last_step and PH < 2:
                continue
            P.wait("vector", sT, pc.get("t_cbt", 0))
            P.wait("vector", sGs, pc.get("tgw", 0))
            P.wait("vector", sV, t_dt)
            tscan = P.op("vector", lambda e: e.tensor_tensor_scan(
                out=cumF, data0=c_ones64.broadcast_to([64, 512]), data1=dtF,
                initial=0.0, op0=ALU.mult, op1=ALU.add), sV)
            P.wait("vector", sV, tscan)
            tlast = P.op("vector", lambda e: e.tensor_scalar_mul(
                out=GinF[:, 0:128], in0=cumF[:, 0:128], scalar1=c_acol), sV)
            for c in range(1, NCH):
                tlast = P.op("vector", lambda e, c=c: e.tensor_scalar(
                    out=GinF[:, c * 128:(c + 1) * 128],
                    in0=cumF[:, c * 128:(c + 1) * 128],
                    scalar1=cumF[:, c * 128 - 1:c * 128], scalar2=c_acol,
                    op0=ALU.subtract, op1=ALU.mult), sV)
            t_inarg = tlast
            for c in range(NCH):
                tlast = P.op("vector", lambda e, c=c: e.tensor_scalar(
                    out=dEndArgF[:, c * 128:(c + 1) * 128],
                    in0=cumF[:, c * 128:(c + 1) * 128],
                    scalar1=cumF[:, c * 128 + 127:c * 128 + 128], scalar2=c_eacol,
                    op0=ALU.subtract, op1=ALU.mult), sV)
            t_endarg = tlast
            P.wait("gpsimd", sV, t_inarg)
            tgw = P.op("gpsimd", lambda e: e.dma_start(
                out=gsc3.rearrange("c (h t) -> h c t", t=128),
                in_=GinF.rearrange("h (c t) -> h c t", t=128)), sGs, 16)
            pc["tgw"] = tgw
            for c in range(NCH):
                cp = c % 2
                P.wait("sync", sGo, t_B_gp)
                if c == 0:
                    P.wait("sync", sT, t_hbuf_cons)
                wait_bank("sync", "XT%d" % cp)
                t_xt = P.op("sync", lambda e, c=c, cp=cp: e.dma_start(
                    out=XTc[:, cp, :], in_=xbd[:, c * 128:(c + 1) * 128], transpose=True), sWt, 16)
                t_xt_reads = sWt.val
                wait_bank("tensor", "B2", "B3")
                P.wait("tensor", sV, t_endarg)
                P.op("tensor", lambda e, c=c: e.transpose(
                    out=ps_gt, in_=GinF[:, c * 128:(c + 1) * 128], identity=c_idf[0:64, 0:64]))
                P.op("tensor", lambda e, c=c: e.transpose(
                    out=ps_dendt, in_=dEndArgF[:, c * 128:(c + 1) * 128],
                    identity=c_idf[0:64, 0:64]))
                P.op("tensor", lambda e, c=c: e.transpose(
                    out=ps_dtt, in_=dtF[:, c * 128:(c + 1) * 128], identity=c_idf[0:64, 0:64]))
                P.wait("tensor", sS, t_BC_scalar)
                P.op("tensor", lambda e, c=c: e.transpose(
                    out=ps_bt, in_=Bb[:, c * 128:(c + 1) * 128], identity=c_idb))
                t_cbt = P.op("tensor", lambda e, c=c: e.matmul(
                    out=ps_cbt, lhsT=Bb[:, c * 128:(c + 1) * 128],
                    rhs=Cb[:, c * 128:(c + 1) * 128], start=True, stop=True), sT)
                P.wait("vector", sT, t_cbt)
                P.wait("vector", sT, pc.get("gmm_last", 0))
                P.wait("vector", sS, pc.get("texpi", 0))
                P.op("vector", lambda e: e.tensor_copy(out=GT, in_=ps_gt))
                P.op("vector", lambda e: e.tensor_copy(out=dEndT, in_=ps_dendt))
                P.op("vector", lambda e: e.tensor_copy(out=dtT, in_=ps_dtt))
                P.op("vector", lambda e: e.tensor_copy(out=BTb, in_=ps_bt))
                t_sm = P.op("vector", lambda e: e.tensor_mul(out=CBm, in0=ps_cbt, in1=c_umask), sV)
                bank_free["B2"] = (sV, t_sm)
                bank_free["B3"] = (sV, t_sm)
                P.wait("scalar", sV, t_sm)
                P.wait("scalar", sGs, pc.get("tgam_c", 0))
                texpi = P.op("scalar", lambda e: e.activation(
                    out=dInT, in_=GT, func=AF.Exp), sS)
                pc["texpi"] = texpi
                texpe = P.op("scalar", lambda e: e.activation(
                    out=dEndT, in_=dEndT, func=AF.Exp), sS)
                P.wait("vector", sS, texpe)
                twE = P.op("vector", lambda e: e.tensor_mul(out=dEndT, in0=dEndT, in1=dtT), sV)
                P.wait("gpsimd", sS, texpi)
                P.wait("gpsimd", sV, t_arB_v)
                tgam_c = P.op("gpsimd", lambda e, c=c: e.dma_start(
                    out=gamdF[c:c + 1, :], in_=dInT[127:128, :]), sGs, 16)
                pc["tgam_c"] = tgam_c
                tgby = P.op("gpsimd", lambda e, c=c: e.dma_start(
                    out=gby, in_=gamdF[c:c + 1, :].partition_broadcast(128).squeeze(1)), sGs, 16)
                P.wait("vector", sV, twE)

                t_group_v = {}
                t_gmm = 0
                if c == 0:
                    P.wait("vector", sS, t_BC_scalar)
                    P.wait("vector", sT, t_hbuf_cons)
                P.wait("vector", sT, pc.get("ytr", 0))
                for g in range(NG):
                    gp = g % 2
                    P.wait("gpsimd", sGs, tgby)
                    if g >= 2:
                        P.wait("gpsimd", sV, t_group_v[g - 2])
                    else:
                        P.wait("gpsimd", sV, t_arB_v)
                    t_grp = P.op("gpsimd", lambda e, c=c, g=g, gp=gp: e.dma_start(
                        out=grpv[gp],
                        in_=gsc3[c:c + 1, g * 1024:(g + 1) * 1024]
                        .partition_broadcast(128).squeeze(1)), sGr, 16)
                    P.wait("vector", sGr, t_grp)
                    P.wait("vector", sV, t_sm)
                    tdb0 = P.op("vector", lambda e, gp=gp, g=g: e.tensor_tensor(
                        out=dbufv.rearrange("p (h t) -> p h t", h=8),
                        in0=grpv[gp].rearrange("p (h t) -> p h t", h=8),
                        in1=GT[:, g * 8:(g + 1) * 8].unsqueeze(2).broadcast_to([128, 8, 128]),
                        op=ALU.subtract), sV)
                    P.wait("vector", sV, tdb0)
                    tdm = P.op("vector", lambda e, gp=gp: e.tensor_scalar_min(
                        out=dminv[gp], in0=dbufv, scalar1=0.0), sV)
                    P.wait("scalar", sV, tdm)
                    texp = P.op("scalar", lambda e, gp=gp: e.activation(
                        out=expdv[gp], in_=dminv[gp], func=AF.Exp), sS)
                    P.wait("vector", sS, texp)
                    P.op("vector", lambda e, gp=gp: e.tensor_tensor(
                        out=mtb[:, gp, :, :],
                        in0=expdv[gp].rearrange("p (h t) -> p h t", h=8),
                        in1=CBm.unsqueeze(1).broadcast_to([128, 8, 128]), op=ALU.mult))
                    if g == 0:
                        P.wait("vector", sWt, t_xt)
                    xtv = XTc[:, cp, :].rearrange("p (h q) -> p h q", h=64)
                    P.op("vector", lambda e, gp=gp, g=g, xtv=xtv: e.tensor_tensor(
                        out=xdt[:, gp, :, :], in0=xtv[:, g * 8:(g + 1) * 8, :],
                        in1=dtT[:, g * 8:(g + 1) * 8].unsqueeze(2).broadcast_to([128, 8, 64]),
                        op=ALU.mult))
                    P.op("vector", lambda e, gp=gp, g=g, xtv=xtv: e.tensor_tensor(
                        out=xwv[:, gp, :, :], in0=xtv[:, g * 8:(g + 1) * 8, :],
                        in1=dEndT[:, g * 8:(g + 1) * 8].unsqueeze(2).broadcast_to([128, 8, 64]),
                        op=ALU.mult))
                    tprep = P.op("vector", lambda e, gp=gp, g=g, xtv=xtv: e.tensor_tensor(
                        out=dxt[:, gp, :, :], in0=xtv[:, g * 8:(g + 1) * 8, :],
                        in1=c_dbc[:, g * 8:(g + 1) * 8].unsqueeze(2).broadcast_to([128, 8, 64]),
                        op=ALU.mult), sV)
                    P.wait("tensor", sV, tprep)
                    wait_bank("tensor", "A%d" % gp, "A%d" % (2 + gp), "B%d" % gp)
                    for j in range(8):
                        P.op("tensor", lambda e, c=c, g=g, j=j, gp=gp: e.matmul(
                            out=bankA(2 + gp)[:, j * 64:(j + 1) * 64],
                            lhsT=Cb[:, c * 128:(c + 1) * 128],
                            rhs=Hbf[:, (8 * g + j) * 64:(8 * g + j + 1) * 64],
                            start=(j == 0), stop=(j == 7)))
                    for j in range(8):
                        P.op("tensor", lambda e, g=g, j=j, gp=gp: e.matmul(
                            out=bankA(gp)[:, j * 64:(j + 1) * 64],
                            lhsT=mtb[:, gp, j, :], rhs=xdt[:, gp, j, :],
                            start=(j == 0), stop=(j == 7)))
                    for j in range(8):
                        fn = (lambda e, g=g, j=j, gp=gp: e.matmul(
                            out=bankB(gp)[:, j * 64:(j + 1) * 64],
                            lhsT=BTb, rhs=xwv[:, gp, j, :],
                            start=(j == 0), stop=(j == 7)))
                        if j == 7:
                            t_gmm = P.op("tensor", fn, sT)
                        else:
                            P.op("tensor", fn)
                    P.wait("vector", sT, t_gmm)
                    if g == 0:
                        P.wait("vector", sGs, tgby)
                    ysl = Ybuf[:, g * 512:(g + 1) * 512].rearrange("p (h q) -> p h q", h=8)
                    ty1 = P.op("vector", lambda e, g=g, gp=gp, ysl=ysl: e.tensor_tensor(
                        out=ysl, in0=bankA(2 + gp).rearrange("p (h q) -> p h q", h=8),
                        in1=dInT[:, g * 8:(g + 1) * 8].unsqueeze(2).broadcast_to([128, 8, 64]),
                        op=ALU.mult), sV)
                    P.wait("vector", sV, ty1)
                    ty2 = P.op("vector", lambda e, gp=gp, ysl=ysl: e.tensor_tensor(
                        out=ysl, in0=ysl,
                        in1=bankA(gp).rearrange("p (h q) -> p h q", h=8), op=ALU.add), sV)
                    P.wait("vector", sV, ty2)
                    P.op("vector", lambda e, gp=gp, ysl=ysl: e.tensor_tensor(
                        out=ysl, in0=ysl, in1=dxt[:, gp, :, :], op=ALU.add))
                    hsl = Hb[:, g * 512:(g + 1) * 512].rearrange("p (h q) -> p h q", h=8)
                    th1 = P.op("vector", lambda e, g=g, hsl=hsl: e.tensor_tensor(
                        out=hsl, in0=hsl,
                        in1=gby[:, g * 8:(g + 1) * 8].unsqueeze(2).broadcast_to([128, 8, 64]),
                        op=ALU.mult), sV)
                    P.wait("vector", sV, th1)
                    th2 = P.op("vector", lambda e, gp=gp, hsl=hsl: e.tensor_tensor(
                        out=hsl, in0=hsl,
                        in1=bankB(gp).rearrange("p (h q) -> p h q", h=8), op=ALU.add), sV)
                    P.wait("vector", sV, th2)
                    t_group_v[g] = P.op("vector", lambda e, g=g: e.tensor_copy(
                        out=Hbf[:, g * 512:(g + 1) * 512],
                        in_=Hb[:, g * 512:(g + 1) * 512]), sV)
                    bank_free["A%d" % gp] = (sV, t_group_v[g])
                    bank_free["A%d" % (2 + gp)] = (sV, t_group_v[g])
                    bank_free["B%d" % gp] = (sV, t_group_v[g])
                bank_free["XT%d" % cp] = (sV, t_group_v[NG - 1])
                t_arB_v = t_group_v[NG - 1]
                pc.update(t_cbt=t_cbt, tgby=tgby, gmm_last=t_gmm)

                P.wait("vector", sV, t_group_v[NG - 1])
                tsq0 = P.op("vector", lambda e: e.scalar_tensor_tensor(
                    out=sqdump, in0=Ybuf, scalar=1.0, in1=Ybuf,
                    op0=ALU.mult, op1=ALU.mult, accum_out=ssq), sV)
                P.wait("vector", sV, tsq0)
                t_sq = P.op("vector", lambda e: e.tensor_scalar(
                    out=lnvar, in0=ssq, scalar1=1.0 / INTER, scalar2=EPS,
                    op0=ALU.mult, op1=ALU.add), sV)
                P.wait("scalar", sV, t_sq)
                tgl0 = P.op("scalar", lambda e: e.activation(out=lnln, in_=lnvar, func=AF.Ln), sS)
                P.wait("scalar", sS, tgl0)
                trst = P.op("scalar", lambda e: e.activation(
                    out=grstd, in_=lnln, func=AF.Exp, scale=-0.5), sS)
                P.wait("vector", sS, trst)
                tys = P.op("vector", lambda e: e.tensor_scalar_mul(
                    out=ysb, in0=Ybuf, scalar1=grstd), sV)
                t_gating_v = tys
                t_arB_v = tys
                P.wait("tensor", sV, tys)
                if c == 0:
                    P.wait("scalar", sT, t_hbuf_cons)
                t_ycp = 0
                for r in range(4):
                    srcs = [ysb[:, (r * 8 + j) * 128:(r * 8 + j + 1) * 128]
                            for j in range(8)]
                    pre_cp = []
                    if r == 0:
                        pre_cp.append(("scalar", sGo, pc.get("t_ybt_dma", 0)))
                    ltr, t_ycp = tr_batch(srcs, "scalar", sS,
                                          dst2d=ybtstg[:, r * 1024:(r + 1) * 1024],
                                          pre_cp=pre_cp)
                pc["ytr"] = ltr
                P.wait("gpsimd", sS, t_ycp)
                if c == 0:
                    P.wait("gpsimd", sWt, t_yb_reads)
                pc["t_ybt_dma"] = P.op("gpsimd", lambda e, c=c: e.dma_start(
                    out=ybdT[:, :, c, :].transpose([1, 0, 2]),
                    in_=ybtstg.rearrange("p (j n) -> p j n", j=32)), sGo, 16)
            t_yb_done = sGo.val

            if last_step and PH < 3:
                continue
            t_ops_w = 0
            t_glv_prev_last = 0
            t_mmfin_prev = 0
            for dh in range(2):
                t_glcons = {}
                t_glv = {}
                t_mmfin = 0
                for j in range(32):
                    jp = j % 4
                    if j % 2 == 0:
                        if j == 0 and dh == 0:
                            P.wait("sync", sGo, t_yb_done)
                            if first:
                                P.wait("sync", sCo, t_wout_cast)
                        if j < 4 and dh == 1:
                            P.wait("sync", sV, t_glv_prev_last)
                            P.wait("sync", sT, t_mmfin_prev)
                        if j >= 4:
                            P.wait("sync", sV, t_glv[j - 3])
                            P.wait("sync", sT, t_glcons[j - 3])
                        tg1 = P.op("sync", lambda e, j=j, jp=jp: e.dma_start(
                            out=glraw[:, jp:jp + 2, :],
                            in_=ybdT[j:j + 2].rearrange("j p c t -> p j (c t)")), sWt, 16)
                        P.op("sync", lambda e, j=j, jp=jp: e.dma_start(
                            out=szw[:, jp:jp + 2, :],
                            in_=szd[j * 128:(j + 2) * 128, :]
                            .rearrange("(j p) t -> p j t", p=128)), sWw, 16)
                        tg3 = P.op("sync", lambda e, j=j, jp=jp, dh=dh: e.dma_start(
                            out=wow4v[:, jp:jp + 2, :],
                            in_=woutb[j * 128:(j + 2) * 128, dh * 1024:(dh + 1) * 1024]
                            .rearrange("(j p) n -> p j n", p=128)), sWw, 16)
                        tg1_pair, tg3_pair = tg1, tg3
                    P.wait("vector", sWt, tg1_pair)
                    P.wait("vector", sWw, tg3_pair)
                    if j >= 4:
                        P.wait("vector", sT, t_glcons[j - 4])
                    elif dh == 1:
                        P.wait("vector", sT, t_mmfin_prev)
                    t_glv[j] = P.op("vector", lambda e, j=j, jp=jp: e.scalar_tensor_tensor(
                        out=glraw[:, jp, :], in0=glraw[:, jp, :],
                        scalar=c_normw[:, j:j + 1], in1=szw[:, jp, :],
                        op0=ALU.mult, op1=ALU.mult), sV)
                    P.wait("tensor", sV, t_glv[j])
                    if j == 0:
                        wait_bank("tensor", *["A%d" % b for b in range(4)],
                                  *["B%d" % b for b in range(4)])
                    for tti in range(4):
                        for dq in range(2):
                            bk = bankA(tti) if dq == 0 else bankB(tti)
                            fn = (lambda e, j=j, jp=jp, tti=tti, dq=dq, bk=bk: e.matmul(
                                out=bk, lhsT=glraw[:, jp, tti * 128:(tti + 1) * 128],
                                rhs=wow4v[:, jp, dq * 512:(dq + 1) * 512],
                                start=(j == 0), stop=(j == 31)))
                            if tti == 3 and dq == 1:
                                t_glcons[j] = P.op("tensor", fn, sT)
                                if j == 31:
                                    t_mmfin = t_glcons[j]
                            else:
                                P.op("tensor", fn)
                t_glv_prev_last = t_glv[31]
                t_mmfin_prev = t_mmfin
                t_yb_reads = sWt.val
                t_sz_reads = sWw.val
                for tti in range(4):
                    for dq in range(2):
                        bk = bankA(tti) if dq == 0 else bankB(tti)
                        P.wait("vector", sT, t_mmfin)
                        if dh == 0 and tti == 0 and dq == 0:
                            P.wait("vector", sGo, t_yb_done)
                        te = P.op("vector", lambda e, bk=bk, tti=tti, dh=dh, dq=dq:
                                  e.tensor_copy(
                                      out=aosb[:, tti * 2048 + dh * 1024 + dq * 512:
                                               tti * 2048 + dh * 1024 + (dq + 1) * 512],
                                      in_=bk), sV)
                        bank_free[("A%d" if dq == 0 else "B%d") % tti] = (sV, te)
            t_x1w = 0
            for tt in range(4):
                P.wait("gpsimd", sV, t_arB_v)
                if tt == 0:
                    P.wait("gpsimd", sGa, t_x1_reads)
                tx = P.op("gpsimd", lambda e, s=s, r0=t0 + tt * 128: e.dma_start(
                    out=xtileB, in_=x_in[s, r0:r0 + 128, :]), sGa, 16)
                if t_x1w:
                    P.wait("vector", sGo, t_x1w)
                trs = ln_stats(aosb[:, tt * 2048:(tt + 1) * 2048], HID)
                P.wait("vector", sS, trs)
                P.wait("vector", sGa, tx)
                tn1 = P.op("vector", lambda e, tt=tt: e.tensor_scalar(
                    out=aosb[:, tt * 2048:(tt + 1) * 2048],
                    in0=aosb[:, tt * 2048:(tt + 1) * 2048],
                    scalar1=lnm, scalar2=lnrstd,
                    op0=ALU.subtract, op1=ALU.mult), sV)
                P.wait("vector", sV, tn1)
                tx1 = P.op("vector", lambda e, tt=tt: e.tensor_add(
                    out=aosb[:, tt * 2048:(tt + 1) * 2048],
                    in0=aosb[:, tt * 2048:(tt + 1) * 2048], in1=xtileB), sV)
                t_arB_v = tx1
                P.wait("gpsimd", sV, tx1)
                t_x1w = P.op("gpsimd", lambda e, tt=tt: e.dma_start(
                    out=x1b[tt * 128:(tt + 1) * 128, :],
                    in_=aosb[:, tt * 2048:(tt + 1) * 2048]), sGo, 16)

            if last_step and PH < 4:
                continue
            tr_hist = {}
            for tt in range(4):
                if tt >= 2:
                    P.wait("vector", sT, tr_hist[tt - 2])
                if tt == 0:
                    P.wait("vector", sGo, t_x1w)
                trs = ln_stats(aosb[:, tt * 2048:(tt + 1) * 2048], HID)
                P.wait("vector", sS, trs)
                tn = P.op("vector", lambda e, tt=tt: e.tensor_scalar(
                    out=xnst[:, tt % 2, :], in0=aosb[:, tt * 2048:(tt + 1) * 2048],
                    scalar1=lnm, scalar2=lnrstd,
                    op0=ALU.subtract, op1=ALU.mult), sV)
                t_arB_v = tn
                P.wait("tensor", sV, tn)
                for r in range(2):
                    srcs = [xnst[:, tt % 2, (r * 8 + k) * 128:(r * 8 + k + 1) * 128]
                            for k in range(8)]
                    pre_cp = []
                    if tt == 0 and r == 0:
                        pre_cp.append(("vector", sT, t_colT_consumed))
                    ltr, t_colT_ready = tr_batch(
                        srcs, "vector", sV,
                        dst3d=colT[:, r * 8:(r + 1) * 8, tt * 128:(tt + 1) * 128],
                        pre_cp=pre_cp)
                tr_hist[tt] = ltr
            t_h = {}
            t_gu = {}
            cap_tr = {}
            cap_wo = {}
            for i in range(64):
                ip = i % 2
                if not first:
                    if i == 0:
                        P.wait("sync", sT, t_mmfin)
                        P.wait("sync", sGo, t_opsw_last)
                        if s == 0 and sc == 1:
                            P.wait("sync", sGo, t_cap_gu_done)
                    if i >= 2:
                        P.wait("sync", sT, t_gu[i - 2])
                    tga = P.op("sync", lambda e, i=i, ip=ip: e.dma_start(
                        out=wguw[ip][0], in_=wgutc[i]), sWw, 16)
                    tgb = P.op("sync", lambda e, i=i, ip=ip: e.dma_start(
                        out=wguw[ip][1], in_=wgutc[i + 64]), sWw, 16)
                    P.wait("tensor", sWw, tgb)
                else:
                    for half in (0, 1):
                        it = i + 64 * half
                        if i >= 1:
                            P.wait("sync", sT, cap_tr[(i - 1, half)])
                        if i == 0 and half == 0:
                            P.wait("sync", sCb, t_wgud_cast)
                        tld = P.op("sync", lambda e, it=it, half=half: e.dma_start(
                            out=stag[half],
                            in_=wgub[it * 128:(it + 1) * 128, :]
                            .rearrange("p (j n) -> p j n", j=16)), sWw, 16)
                        P.wait("tensor", sWw, tld)
                        t_ccp = 0
                        for r in range(2):
                            srcs = [stag[half][:, r * 8 + j, :] for j in range(8)]
                            pre_cp = []
                            if r == 0:
                                if i >= 2:
                                    pre_cp += [("vector", sT, t_gu[i - 2]),
                                               ("vector", sGo, cap_wo[(i - 2, half)])]
                                else:
                                    pre_cp += [("vector", sT, t_mmfin),
                                               ("vector", sGo, t_opsw_last)]
                            ltr, t_ccp = tr_batch(
                                srcs, "vector", sV,
                                dst2d=wguwF[ip][half][:, r * 1024:(r + 1) * 1024],
                                pre_cp=pre_cp)
                        cap_tr[(i, half)] = ltr
                        P.wait("gpsimd", sV, t_ccp)
                        cap_wo[(i, half)] = P.op("gpsimd", lambda e, it=it, ip=ip, half=half:
                                                 e.dma_start(out=wgutc[it],
                                                             in_=wguw[ip][half]), sGo, 16)
                    t_cap_gu_done = cap_wo[(63, 1)] if i == 63 else t_cap_gu_done
                    P.wait("tensor", sV, t_ccp)
                if i == 0:
                    P.wait("tensor", sV, t_colT_ready)
                wait_bank("tensor", "A%d" % ip, "A%d" % (2 + ip))
                for k in range(16):
                    P.op("tensor", lambda e, ip=ip, k=k: e.matmul(
                        out=bankA(ip), lhsT=wguw[ip][0][:, k, :], rhs=colT[:, k, :],
                        start=(k == 0), stop=(k == 15)))
                for k in range(16):
                    fn = (lambda e, ip=ip, k=k: e.matmul(
                        out=bankA(2 + ip), lhsT=wguw[ip][1][:, k, :], rhs=colT[:, k, :],
                        start=(k == 0), stop=(k == 15)))
                    if k == 15:
                        t_gu[i] = P.op("tensor", fn, sT)
                    else:
                        P.op("tensor", fn)
                P.wait("scalar", sT, t_gu[i])
                if i >= 2:
                    P.wait("scalar", sV, t_h[i - 2])
                tsg = P.op("scalar", lambda e, ip=ip: e.activation(
                    out=sgst[:, ip, :], in_=bankA(ip), func=AF.Silu), sS)
                bank_free["A%d" % ip] = (sS, tsg)
                P.wait("vector", sS, tsg)
                if i == 0:
                    P.wait("vector", sT, t_hbuf_cons)
                t_h[i] = P.op("vector", lambda e, i=i, ip=ip: e.tensor_tensor(
                    out=hbuf[:, i, :], in0=sgst[:, ip, :], in1=bankA(2 + ip),
                    op=ALU.mult), sV)
                bank_free["A%d" % (2 + ip)] = (sV, t_h[i])
            t_colT_consumed = t_gu[63]
            t_dn = {}
            t_dn_half = {}
            t_ff_w = 0
            t_fftr = {}
            t_ffw = {}
            cap_dtr = {}
            cap_dwo = {}
            for d in range(16):
                dp = d % 2
                for qr in range(4):
                    hb = (4 * d + qr) % 2
                    q_idx = 4 * d + qr
                    if not first:
                        if q_idx == 0:
                            P.wait("sync", sT, t_hbuf_cons)
                            if s == 0 and sc == 1:
                                P.wait("sync", sGo, t_cap_dn_done)
                        if q_idx >= 2:
                            P.wait("sync", sT, t_dn_half[q_idx - 2])
                        twd = P.op("sync", lambda e, d=d, qr=qr, hb=hb: e.dma_start(
                            out=wdw[hb], in_=wdtc[d, :, qr * 16:(qr + 1) * 16, :]), sWw, 16)
                        P.wait("tensor", sWw, twd)
                    else:
                        if q_idx >= 2:
                            P.wait("sync", sT, cap_dtr[q_idx - 2])
                        else:
                            P.wait("sync", sT, cap_tr[(63, hb)])
                        tld = P.op("sync", lambda e, d=d, qr=qr: e.dma_start(
                            out=stag[(4 * d + qr) % 2],
                            in_=wdb[d * 128:(d + 1) * 128, qr * 2048:(qr + 1) * 2048]
                            .rearrange("p (j n) -> p j n", j=16)), sWw, 16)
                        P.wait("tensor", sWw, tld)
                        t_ccp = 0
                        for r in range(2):
                            srcs = [stag[hb][:, r * 8 + j, :] for j in range(8)]
                            pre_cp = []
                            if r == 0:
                                if q_idx >= 2:
                                    pre_cp += [("vector", sT, t_dn_half[q_idx - 2]),
                                               ("vector", sGo, cap_dwo[q_idx - 2])]
                                else:
                                    pre_cp += [("vector", sT, t_hbuf_cons)]
                            ltr, t_ccp = tr_batch(
                                srcs, "vector", sV,
                                dst2d=wdwF[hb][:, r * 1024:(r + 1) * 1024],
                                pre_cp=pre_cp)
                        cap_dtr[q_idx] = ltr
                        P.wait("gpsimd", sV, t_ccp)
                        cap_dwo[q_idx] = P.op("gpsimd", lambda e, d=d, qr=qr, hb=hb:
                                              e.dma_start(out=wdtc[d, :, qr * 16:(qr + 1) * 16, :],
                                                          in_=wdw[hb]), sGo, 16)
                        if q_idx == 63:
                            t_cap_dn_done = cap_dwo[q_idx]
                        P.wait("tensor", sV, t_ccp)
                    if d == 0 and qr == 0:
                        P.wait("tensor", sV, t_h[63])
                        wait_bank("tensor", "B0", "B1")
                    if qr == 0:
                        wait_bank("tensor", "B%d" % dp)
                    for ii in range(16):
                        iglob = qr * 16 + ii
                        fn = (lambda e, d=d, dp=dp, hb=hb, ii=ii, iglob=iglob: e.matmul(
                            out=bankB(dp), lhsT=wdw[hb][:, ii, :], rhs=hbuf[:, iglob, :],
                            start=(iglob == 0), stop=(iglob == 63)))
                        if ii == 15:
                            tlast = P.op("tensor", fn, sT)
                            t_dn_half[4 * d + qr] = tlast
                            if qr == 3:
                                t_dn[d] = tlast
                        else:
                            P.op("tensor", fn)
                P.wait("vector", sT, t_dn[d])
                P.wait("vector", sT, t_fftr.get(dp, 0))
                tfe = P.op("vector", lambda e, d=d, dp=dp: e.tensor_copy(
                    out=ffst[:, dp, :], in_=bankB(dp)), sV)
                bank_free["B%d" % dp] = (sV, tfe)
                P.wait("tensor", sV, tfe)
                pre_cp = [("scalar", sGo, t_ffw.get(dp, 0))]
                if first and d == 0:
                    pre_cp.append(("scalar", sGo, t_cap_gu_done))
                srcs = [ffst[:, dp, tt * 128:(tt + 1) * 128] for tt in range(4)]
                ltr, tcp = tr_batch(srcs, "scalar", sS, dst2d=stagF[dp], pre_cp=pre_cp)
                t_fftr[dp] = ltr
                P.wait("gpsimd", sS, tcp)
                t_ff_w = P.op("gpsimd", lambda e, d=d, dp=dp: e.dma_start(
                    out=ffbdT[:, :, d, :].transpose([1, 0, 2]),
                    in_=stagF[dp].rearrange("p (c n) -> p c n", c=4)), sGo, 16)
                t_ffw[dp] = t_ff_w
            t_hbuf_cons = t_dn[15]
            t_ln4v = {}
            for tt in range(4):
                ftp = tt % 2
                P.wait("sync", sGo, t_ff_w)
                if tt >= 2:
                    P.wait("sync", sV, t_ln4v[tt - 2])
                t_fft = P.op("sync", lambda e, tt=tt, ftp=ftp: e.dma_start(
                    out=fftl[ftp], in_=ffbdT[tt].rearrange("p d n -> p (d n)")), sWt, 16)
                P.wait("gpsimd", sV, t_arB_v)
                tx = P.op("gpsimd", lambda e, tt=tt: e.dma_start(
                    out=xtileA, in_=x1b[tt * 128:(tt + 1) * 128, :]), sGa, 16)
                P.wait("vector", sWt, t_fft)
                if t_yout_last:
                    P.wait("vector", sGo, t_yout_last)
                trs = ln_stats(fftl[ftp], HID)
                P.wait("vector", sS, trs)
                P.wait("vector", sGa, tx)
                tn1 = P.op("vector", lambda e, ftp=ftp: e.tensor_scalar(
                    out=outstg[:, 0:2048], in0=fftl[ftp], scalar1=lnm, scalar2=lnrstd,
                    op0=ALU.subtract, op1=ALU.mult), sV)
                P.wait("vector", sV, tn1)
                t_ln4v[tt] = P.op("vector", lambda e: e.tensor_add(
                    out=outstg[:, 0:2048], in0=outstg[:, 0:2048], in1=xtileA), sV)
                t_arB_v = t_ln4v[tt]
                P.wait("gpsimd", sV, t_ln4v[tt])
                t_yout_last = P.op("gpsimd", lambda e, s=s, r0=t0 + tt * 128: e.dma_start(
                    out=y_out[s, r0:r0 + 128, :], in_=outstg[:, 0:2048]), sGo, 16)
            t_x1_reads = sGa.val

    P.wait("gpsimd", sGo, sGo.val)
    P.wait("gpsimd", sGa, sGa.val)
    P.wait("gpsimd", sGr, sGr.val)
    P.wait("gpsimd", sGs, sGs.val)
    P.wait("gpsimd", sCa, sCa.val)
    P.wait("gpsimd", sCo, sCo.val)
    P.wait("gpsimd", sCb, sCb.val)
    P.wait("sync", sWw, sWw.val)
    P.wait("sync", sWt, sWt.val)

    blk = ctx.enter_context(nc.Block())
    for eng in ("sync", "gpsimd", "scalar", "vector", "tensor"):
        def runner(e, ops=tuple(P.q[eng])):
            for f in ops:
                f(e)
        getattr(blk, eng)(runner)
    ctx.close()
    return nc


_NC_CACHE = None


def kernel(**inputs):
    global _NC_CACHE
    x = np.ascontiguousarray(inputs["L0_attn_hidden_states"])
    consts = dict(
        idf_i=np.eye(128, dtype=np.float32),
        idb_i=np.eye(128, dtype=ml_dtypes.bfloat16),
        umask_i=np.triu(np.ones((128, 128), np.float32)),
    )
    weights = dict(
        w1=np.ascontiguousarray(inputs["L0_attn_in_proj_w"], dtype=np.float32),
        convw_i=np.ascontiguousarray(inputs["L0_attn_conv_w"], dtype=np.float32),
        convb_i=np.ascontiguousarray(inputs["L0_attn_conv_b"], dtype=np.float32).reshape(34, 128),
        alog_i=np.ascontiguousarray(inputs["L0_attn_A_log"], dtype=np.float32).reshape(1, 64),
        dvec_i=np.ascontiguousarray(inputs["L0_attn_D"], dtype=np.float32).reshape(1, 64),
        dtb_i=np.ascontiguousarray(inputs["L0_attn_dt_bias"], dtype=np.float32).reshape(1, 64),
        normw_i=np.ascontiguousarray(inputs["L0_attn_norm_w"], dtype=np.float32).reshape(32, 128),
        wout=np.ascontiguousarray(inputs["L0_attn_out_proj_w"], dtype=np.float32),
        wgu=np.ascontiguousarray(inputs["L0_ffn_w_gate_up"], dtype=np.float32),
        wdn=np.ascontiguousarray(inputs["L0_ffn_w_down"], dtype=np.float32),
    )
    if _NC_CACHE is None:
        _NC_CACHE = build_nc()
    nc = _NC_CACHE
    in_maps = []
    for core in range(NCORES):
        m = dict(consts)
        m.update(weights)
        m["x_in"] = np.ascontiguousarray(x[2 * core:2 * core + 2])
        in_maps.append(m)
    res = run_bass_kernel_spmd(nc, in_maps, core_ids=list(range(NCORES)))
    out = np.empty((16, 2048, HID), np.float32)
    for core in range(NCORES):
        out[2 * core:2 * core + 2] = res.results[core]["y_out"]
    return out


# version 15 (final)
# speedup vs baseline: 1.0293x; 1.0042x over previous
import numpy as np
import ml_dtypes
import concourse.bass as bass
import concourse.mybir as mybir
from concourse.bass_utils import run_bass_kernel_spmd

F32 = mybir.dt.float32
BF16 = mybir.dt.bfloat16
AF = mybir.ActivationFunctionType
ALU = mybir.AluOpType
AX = mybir.AxisListType

NCORES = 8
HID = 2048
PROJ = 8512
INTER = 4096
NFT = 67
HEADS = 64
Q = 128
SC = 512
NCH = SC // Q
NSC = 2048 // SC
NSEQ = 2
NG = 8
FFI = 8192
EPS = 1e-5


class Sem:
    def __init__(self, ctx, nc, name):
        self.h = ctx.enter_context(nc.semaphore(name))
        self.val = 0

    def tick(self, k=1):
        self.val += k
        return self.val


class Prog:
    def __init__(self):
        self.q = {k: [] for k in ("sync", "gpsimd", "scalar", "vector", "tensor")}

    def wait(self, eng, sem, val):
        if val and val > 0:
            self.q[eng].append(lambda e, s=sem.h, v=val: e.wait_ge(s, v))

    def op(self, eng, fn, sem=None, k=1, serialize=True):
        if sem is None:
            self.q[eng].append(fn)
            return None
        if k == 16 and serialize:
            self.wait(eng, sem, sem.val)
        t = sem.tick(k)
        self.q[eng].append(lambda e, f=fn, s=sem.h, kk=k: f(e).then_inc(s, kk))
        return t


def build_nc(n_steps=None, phase_limit="E", debug=False):
    import contextlib
    nc = bass.Bass("TRN2", target_bir_lowering=False)
    ctx = contextlib.ExitStack()
    PH = "ABCDE".index(phase_limit)
    kind_dbg = "ExternalOutput" if debug else "Internal"


    x_in = nc.dram_tensor("x_in", [NSEQ, 2048, HID], F32, kind="ExternalInput").ap()
    w1 = nc.dram_tensor("w1", [HID, PROJ], F32, kind="ExternalInput").ap()
    convw_i = nc.dram_tensor("convw_i", [4352, 4], F32, kind="ExternalInput").ap()
    convb_i = nc.dram_tensor("convb_i", [34, 128], F32, kind="ExternalInput").ap()
    alog_i = nc.dram_tensor("alog_i", [1, 64], F32, kind="ExternalInput").ap()
    dvec_i = nc.dram_tensor("dvec_i", [1, 64], F32, kind="ExternalInput").ap()
    dtb_i = nc.dram_tensor("dtb_i", [1, 64], F32, kind="ExternalInput").ap()
    normw_i = nc.dram_tensor("normw_i", [32, 128], F32, kind="ExternalInput").ap()
    wout = nc.dram_tensor("wout", [INTER, HID], F32, kind="ExternalInput").ap()
    wgu = nc.dram_tensor("wgu", [2 * FFI, HID], F32, kind="ExternalInput").ap()
    wdn = nc.dram_tensor("wdn", [HID, FFI], F32, kind="ExternalInput").ap()
    idf_i = nc.dram_tensor("idf_i", [128, 128], F32, kind="ExternalInput").ap()
    idb_i = nc.dram_tensor("idb_i", [128, 128], BF16, kind="ExternalInput").ap()
    umask_i = nc.dram_tensor("umask_i", [128, 128], F32, kind="ExternalInput").ap()

    y_out = nc.dram_tensor("y_out", [NSEQ, 2048, HID], F32, kind="ExternalOutput").ap()

    w1t = nc.dram_tensor("w1t", [NFT, 128, 16, 128], BF16).ap()
    woutb = nc.dram_tensor("woutb", [INTER, HID], BF16).ap()
    wgub = nc.dram_tensor("wgub", [2 * FFI, HID], BF16).ap()
    wgutc = nc.dram_tensor("wgutc", [128, 128, 16, 128], BF16).ap()
    wdb = nc.dram_tensor("wdb", [HID, FFI], BF16).ap()
    wdtc = nc.dram_tensor("wdtc", [16, 128, 64, 128], BF16).ap()
    xbd = nc.dram_tensor("xbd", [INTER, SC], BF16, kind=kind_dbg).ap()
    szd = nc.dram_tensor("szd", [INTER, SC], BF16, kind=kind_dbg).ap()
    ybdT = nc.dram_tensor("ybdT", [32, 128, NCH, 128], BF16).ap()
    x1b = nc.dram_tensor("x1b", [SC, HID], F32, kind=kind_dbg).ap()
    ffbdT = nc.dram_tensor("ffbdT", [4, 128, 16, 128], BF16).ap()
    gsc3 = nc.dram_tensor("gsc3", [NCH, 8192], F32).ap()
    gamdF = nc.dram_tensor("gamdF", [NCH, 64], F32).ap()

    def T(name, shape, dt=F32):
        return ctx.enter_context(nc.sbuf_tensor(name, shape, dt))[:]

    c_idf = T("c_idf", [128, 128])
    c_idb = T("c_idb", [128, 128], BF16)
    c_umask = T("c_umask", [128, 128])
    c_convw = T("c_convw", [128, 34, 4])
    c_convb = T("c_convb", [128, 34])
    c_normw = T("c_normw", [128, 32])
    c_dbc = T("c_dbc", [128, 64])
    c_ones64 = T("c_ones64", [64, 1])
    c_acol = T("c_acol", [64, 1])
    c_eacol = T("c_eacol", [64, 1])
    c_dtb = T("c_dtb", [64, 1])
    st_rows = T("st_rows", [34, 128])
    s64 = T("s64", [64, 4])

    Hb = T("Hb", [128, INTER])
    Hbf = T("Hbf", [128, INTER], BF16)
    convh = T("convh", [128, NSEQ, 34, 3])
    colT = T("colT", [128, 16, SC], BF16)
    Bb = T("Bb", [128, SC], BF16)
    Cb = T("Cb", [128, SC], BF16)
    BTb = T("BTb", [128, 128], BF16)
    dtF = T("dtF", [64, SC])

    arA = T("arA", [128, 18432])
    arB = T("arB", [128, 7168])
    arC = T("arC", [128, 6144])

    prepP1 = T("prepP1", [128, 512])
    prepP2 = T("prepP2", [128, 512])
    GinF = prepP1[0:64, :]
    cumF = prepP1[64:128, :]
    dEndArgF = prepP2[0:64, :]
    GT = T("GT", [128, 64])
    dInT = T("dInT", [128, 64])
    dEndT = T("dEndT", [128, 64])
    dtT = T("dtT", [128, 64])
    gby = T("gby", [128, 64])
    CBm = T("CBm", [128, 128])
    mtb = T("mtb", [128, 2, 8, 128], BF16)
    xdt = T("xdt", [128, 2, 8, 64], BF16)
    xwv = T("xwv", [128, 2, 8, 64], BF16)
    dxt = T("dxt", [128, 2, 8, 64])

    xnst = T("xnst", [128, 2, HID], BF16)
    xsil = T("xsil", [128, 2, SC], BF16)
    glraw = T("glraw", [128, 4, SC], BF16)
    szw = T("szw", [128, 4, SC], BF16)
    smalls = T("smalls", [128, 16])

    psA = ctx.enter_context(nc.psum_tensor("psA", [128, 2048], F32))[:]
    psB = ctx.enter_context(nc.psum_tensor("psB", [128, 2048], F32))[:]

    hbuf = arA[:, 0:16384].bitcast(BF16).rearrange("p (i t) -> p i t", i=64)
    Ybuf = arA[:, 0:4096]
    XTc = arA[:, 4096:8192].bitcast(BF16).rearrange("p (c t) -> p c t", c=2)
    stag = [arA[:, 16384:17408].bitcast(BF16).rearrange("p (j n) -> p j n", j=16),
            arA[:, 17408:18432].bitcast(BF16).rearrange("p (j n) -> p j n", j=16)]
    ybtstg = arA[:, 8192:10240].bitcast(BF16)
    aosb = arA[:, 8192:16384]
    sgst = arB[:, 6144:7168].rearrange("p (a t) -> p a t", a=2)
    cacc = arB[:, 2048:3072].rearrange("p (a t) -> p a t", a=2)
    grpv = [arB[:, 0:1024], arB[:, 1024:2048]]
    dbufv = arB[:, 2048:3072]
    dminv = [arB[:, 3072:4096], arB[:, 4096:5120]]
    expdv = [arB[:, 5120:6144], arB[:, 6144:7168]]
    sqdump = arB[:, 0:4096]
    ysb = arB[:, 4096:6144].bitcast(BF16)
    xtileA = arB[:, 0:2048]
    xtileB = arB[:, 2048:4096]
    lndump = arB[:, 4096:6144]
    outstg = arB[:, 4096:6144]
    dt_t1 = arB[:, 0:512][0:64, :]
    dt_t2 = arB[:, 512:1024][0:64, :]
    dt_t3 = arB[:, 1024:1536][0:64, :]

    w1w = [arC[:, 0:1024].bitcast(BF16).rearrange("p (k n) -> p k n", k=16),
           arC[:, 1024:2048].bitcast(BF16).rearrange("p (k n) -> p k n", k=16)]
    wow4v = arC[:, 2048:4096].bitcast(BF16).rearrange("p (j n) -> p j n", j=4)
    stagF = [arC[:, 2048:2304].bitcast(BF16), arC[:, 2304:2560].bitcast(BF16)]
    wguw = [[arC[:, 0:1024].bitcast(BF16).rearrange("p (k n) -> p k n", k=16),
             arC[:, 1024:2048].bitcast(BF16).rearrange("p (k n) -> p k n", k=16)],
            [arC[:, 2048:3072].bitcast(BF16).rearrange("p (k n) -> p k n", k=16),
             arC[:, 3072:4096].bitcast(BF16).rearrange("p (k n) -> p k n", k=16)]]
    wdw = [arC[:, 4096:5120].bitcast(BF16).rearrange("p (i n) -> p i n", i=16),
           arC[:, 5120:6144].bitcast(BF16).rearrange("p (i n) -> p i n", i=16)]
    wguwF = [[arC[:, 0:1024].bitcast(BF16), arC[:, 1024:2048].bitcast(BF16)],
             [arC[:, 2048:3072].bitcast(BF16), arC[:, 3072:4096].bitcast(BF16)]]
    wdwF = [arC[:, 4096:5120].bitcast(BF16), arC[:, 5120:6144].bitcast(BF16)]
    fftl = [arC[:, 0:1024].bitcast(BF16), arC[:, 1024:2048].bitcast(BF16)]
    xbcp = arB[:, 3072:4102].rearrange("p (a t) -> p a t", a=2)
    ffst = arB[:, 6144:7168].bitcast(BF16).rearrange("p (a t) -> p a t", a=2)[:, :, 0:512]

    lnsum = smalls[:, 0:1]
    lnm = smalls[:, 1:2]
    lnmsq = smalls[:, 2:3]
    lnvar = smalls[:, 3:4]
    lnln = smalls[:, 4:5]
    lnrstd = smalls[:, 5:6]
    ssq = smalls[:, 6:7]
    grstd = smalls[:, 7:8]

    def bankA(i):
        return psA[:, i * 512:(i + 1) * 512]

    def bankB(i):
        return psB[:, i * 512:(i + 1) * 512]

    ps_cbt = psB[:, 1024:1152]
    ps_gt = psB[:, 1152:1216]
    ps_dint = psB[:, 1216:1280]
    ps_dendt = psB[:, 1280:1344]
    ps_dtt = psB[:, 1344:1408]
    ps_bt = psB[:, 1536:1600].bitcast(BF16)
    psTB = [psB[:, 1024:1536].bitcast(BF16), psB[:, 1536:2048].bitcast(BF16)]

    P = Prog()
    sWw = Sem(ctx, nc, "sWw")
    sWt = Sem(ctx, nc, "sWt")
    sGa = Sem(ctx, nc, "sGa")
    sGo = Sem(ctx, nc, "sGo")
    sGr = Sem(ctx, nc, "sGr")
    sGs = Sem(ctx, nc, "sGs")
    sT = Sem(ctx, nc, "sT")
    sV = Sem(ctx, nc, "sV")
    sS = Sem(ctx, nc, "sS")
    sCa = Sem(ctx, nc, "sCa")
    sCo = Sem(ctx, nc, "sCo")
    sCb = Sem(ctx, nc, "sCb")

    bank_free = {}

    def wait_bank(eng, *keys):
        for key in keys:
            if key in bank_free:
                sem, val = bank_free.pop(key)
                P.wait(eng, sem, val)

    tb_rot = [0]

    def tr_batch(srcs, copy_eng, copy_sem, dst2d=None, dst3d=None, pre_cp=()):
        bank = tb_rot[0]
        tb_rot[0] ^= 1
        wait_bank("tensor", "B%d" % (2 + bank))
        last_tr = None
        for idx, s0 in enumerate(srcs):
            fn = (lambda e, s0=s0, idx=idx, bank=bank: e.transpose(
                out=psTB[bank][:, idx * 128:(idx + 1) * 128], in_=s0, identity=c_idb))
            last_tr = P.op("tensor", fn, sT) if idx == len(srcs) - 1 else P.op("tensor", fn)
        for (weng, wsem, wval) in pre_cp:
            P.wait(copy_eng, wsem, wval)
        P.wait(copy_eng, sT, last_tr)
        n = len(srcs)
        if copy_eng == "scalar":
            tcp = P.op("scalar", lambda e, bank=bank, n=n, dst2d=dst2d: e.activation(
                out=dst2d, in_=psTB[bank][:, 0:n * 128], func=AF.Copy), copy_sem)
        elif dst2d is not None:
            tcp = P.op("vector", lambda e, bank=bank, n=n, dst2d=dst2d: e.tensor_copy(
                out=dst2d, in_=psTB[bank][:, 0:n * 128]), copy_sem)
        else:
            src_ap = psTB[bank][:, 0:n * 128].rearrange("p (j n) -> p j n", j=n)
            tcp = P.op("vector", lambda e, dst3d=dst3d, src_ap=src_ap: e.tensor_copy(
                out=dst3d, in_=src_ap), copy_sem)
        bank_free["B%d" % (2 + bank)] = (copy_sem, tcp)
        return last_tr, tcp

    for dst, src in ((c_idf, idf_i), (c_idb, idb_i), (c_umask, umask_i)):
        P.op("gpsimd", lambda e, d=dst, s=src: e.dma_start(out=d, in_=s), sGs, 16)
    P.op("gpsimd", lambda e: e.dma_start(
        out=c_convw, in_=convw_i.rearrange("(j p) k -> p j k", p=128)), sGs, 16)
    P.op("gpsimd", lambda e: e.dma_start(out=st_rows, in_=convb_i), sGs, 16)
    t_cb_rows = sGs.val
    P.op("gpsimd", lambda e: e.dma_start(
        out=c_dbc, in_=dvec_i.partition_broadcast(128).squeeze(1)), sGs, 16)
    P.op("gpsimd", lambda e: e.dma_start(out=s64[0:64, 0:1], in_=alog_i.transpose([1, 0])), sGs, 16)
    P.op("gpsimd", lambda e: e.dma_start(out=s64[0:64, 1:2], in_=dtb_i.transpose([1, 0])), sGs, 16)
    t_small_rows = sGs.val
    P.wait("tensor", sGs, t_small_rows)
    t1 = P.op("tensor", lambda e: e.transpose(
        out=psB[:, 0:34], in_=st_rows, identity=c_idf[0:34, 0:34]), sT)
    P.wait("vector", sT, t1)
    t_cb_done = P.op("vector", lambda e: e.tensor_copy(out=c_convb, in_=psB[:, 0:34]), sV)
    P.wait("gpsimd", sV, t_cb_done)
    P.op("gpsimd", lambda e: e.dma_start(out=st_rows[0:32, :], in_=normw_i), sGs, 16)
    t_nw_rows = sGs.val
    P.wait("tensor", sGs, t_nw_rows)
    t2 = P.op("tensor", lambda e: e.transpose(
        out=psB[:, 64:96], in_=st_rows[0:32, :], identity=c_idf[0:32, 0:32]), sT)
    P.wait("vector", sT, t2)
    t_nw_done = P.op("vector", lambda e: e.tensor_copy(out=c_normw, in_=psB[:, 64:96]), sV)
    bank_free["B0"] = (sV, t_nw_done)
    P.wait("scalar", sGs, t_nw_rows)
    t3 = P.op("scalar", lambda e: e.activation(out=c_eacol, in_=s64[0:64, 0:1], func=AF.Exp), sS)
    P.wait("vector", sS, t3)
    P.op("vector", lambda e: e.tensor_scalar_mul(out=c_acol, in0=c_eacol, scalar1=-1.0))
    P.op("vector", lambda e: e.tensor_copy(out=c_dtb, in_=s64[0:64, 1:2]))
    P.op("vector", lambda e: e.memset(c_ones64, 1.0))
    P.op("vector", lambda e: e.memset(Hb, 0.0))
    P.op("vector", lambda e: e.memset(Hbf, 0.0))
    P.op("vector", lambda e: e.memset(convh, 0.0), sV)

    W1CH = [(0, 9), (9, 18), (18, 27), (27, 36), (36, 45), (45, 54), (54, 63), (63, 67)]

    def emit_w1_chunk(k):
        f0, f1 = W1CH[k]
        for f in range(f0, f1):
            w = 128 if f < NFT - 1 else 64
            P.op("gpsimd", lambda e, f=f, w=w: e.dma_start(
                out=w1t[f, :, :, 0:w],
                in_=w1[:, f * 128:f * 128 + w].rearrange("(k p) n -> p k n", p=128)),
                sCa, 16)

    def emit_rest_casts():
        for i in range(4):
            P.op("gpsimd", lambda e, i=i: e.dma_start(
                out=woutb[i * 1024:(i + 1) * 1024, :], in_=wout[i * 1024:(i + 1) * 1024, :]),
                sCo, 16, serialize=False)
        for i in range(8):
            P.op("gpsimd", lambda e, i=i: e.dma_start(
                out=wgub[i * 2048:(i + 1) * 2048, :], in_=wgu[i * 2048:(i + 1) * 2048, :]),
                sCb, 16, serialize=False)
        for i in range(4):
            P.op("gpsimd", lambda e, i=i: e.dma_start(
                out=wdb[:, i * 2048:(i + 1) * 2048], in_=wdn[:, i * 2048:(i + 1) * 2048]),
                sCb, 16, serialize=False)

    t_w1_cast = 16 * NFT
    t_wout_cast = 4 * 16
    t_wgud_cast = 12 * 16

    sp_v0 = smalls[:, 12:13]
    sp_v1 = smalls[:, 13:14]
    sp_s0 = smalls[:, 14:15]
    sp_s1 = smalls[:, 15:16]

    def v_spacer():
        P.op("vector", lambda e: e.tensor_copy(out=sp_v0, in_=sp_v1))

    def s_spacer():
        P.op("scalar", lambda e: e.activation(out=sp_s0, in_=sp_s1, func=AF.Identity))

    def ln_stats(xtile, F):
        P.op("vector", lambda e, xtile=xtile: e.reduce_sum(out=lnsum, in_=xtile, axis=AX.X))
        t2 = P.op("vector", lambda e, xtile=xtile, F=F: e.scalar_tensor_tensor(
            out=lndump[:, 0:F], in0=xtile, scalar=1.0, in1=xtile,
            op0=ALU.mult, op1=ALU.mult, accum_out=ssq), sV)
        P.wait("vector", sV, t2)
        P.op("vector", lambda e, F=F: e.tensor_scalar_mul(out=lnm, in0=lnsum, scalar1=1.0 / F))
        t3 = P.op("vector", lambda e, F=F: e.tensor_scalar(
            out=lnvar, in0=ssq, scalar1=1.0 / F, scalar2=EPS, op0=ALU.mult, op1=ALU.add), sV)
        P.wait("vector", sV, t3)
        tm = P.op("vector", lambda e: e.tensor_mul(out=lnmsq, in0=lnm, in1=lnm), sV)
        P.wait("vector", sV, tm)
        t = P.op("vector", lambda e: e.tensor_sub(out=lnvar, in0=lnvar, in1=lnmsq), sV)
        P.wait("scalar", sV, t)
        ts5 = P.op("scalar", lambda e: e.activation(out=lnln, in_=lnvar, func=AF.Ln), sS)
        P.wait("scalar", sS, ts5)
        return P.op("scalar", lambda e: e.activation(
            out=lnrstd, in_=lnln, func=AF.Exp, scale=-0.5), sS)

    t_colT_consumed = 0
    t_arB_v = sV.val
    t_xt_reads = 0
    t_yb_reads = 0
    t_sz_reads = 0
    t_x1_reads = 0
    t_aob_reads = 0
    t_hbuf_cons = 0
    t_mmfin = 0
    t_opsw_last = 0
    t_gating_v = 0
    t_dt = 0
    t_yout_last = 0
    pc = {}
    t_cap_gu_done = 0
    t_cap_dn_done = 0

    step = 0
    stop_all = False
    for s in range(NSEQ):
        if stop_all:
            break
        if s > 0:
            P.wait("vector", sT, sT.val)
            P.op("vector", lambda e: e.memset(Hb, 0.0))
            tmr = P.op("vector", lambda e: e.memset(Hbf, 0.0), sV)
        for sc in range(NSC):
            if n_steps is not None and step >= n_steps:
                stop_all = True
                break
            step += 1
            last_step = (n_steps is not None and step == n_steps)
            t0 = sc * SC
            first = (s == 0 and sc == 0)
            tr_hist = {}
            for tt in range(4):
                xt = xtileA if tt % 2 == 0 else xtileB
                P.wait("gpsimd", sV, t_arB_v)
                if first:
                    emit_w1_chunk(2 * tt)
                tl = P.op("gpsimd", lambda e, s=s, r0=t0 + tt * 128, xt=xt: e.dma_start(
                    out=xt, in_=x_in[s, r0:r0 + 128, :]), sGa, 16)
                if first:
                    emit_w1_chunk(2 * tt + 1)
                    if tt == 3:
                        emit_rest_casts()
                P.wait("vector", sGa, tl)
                if tt >= 2:
                    P.wait("vector", sT, tr_hist[tt - 2])
                if tt == 0:
                    P.wait("vector", sGo, t_yout_last)
                trs = ln_stats(xt, HID)
                P.wait("vector", sS, trs)
                tn = P.op("vector", lambda e, xt=xt, tt=tt: e.tensor_scalar(
                    out=xnst[:, tt % 2, :], in0=xt, scalar1=lnm, scalar2=lnrstd,
                    op0=ALU.subtract, op1=ALU.mult), sV)
                t_arB_v = tn
                P.wait("tensor", sV, tn)
                for r in range(2):
                    srcs = [xnst[:, tt % 2, (r * 8 + k) * 128:(r * 8 + k + 1) * 128]
                            for k in range(8)]
                    pre_cp = []
                    if tt == 0 and r == 0:
                        pre_cp.append(("vector", sT, t_colT_consumed))
                    ltr, t_colT_ready = tr_batch(
                        srcs, "vector", sV,
                        dst3d=colT[:, r * 8:(r + 1) * 8, tt * 128:(tt + 1) * 128],
                        pre_cp=pre_cp)
                tr_hist[tt] = ltr

            if last_step and PH < 1:
                continue
            t_mm = {}
            t_xsil_dma = {0: 0, 1: 0}
            t_conv_v = {0: 0, 1: 0}
            P.wait("gpsimd", sWt, t_xt_reads)
            P.wait("gpsimd", sWw, t_sz_reads)
            for f in range(NFT):
                w = 128 if f < NFT - 1 else 64
                if f >= 2:
                    P.wait("sync", sT, t_mm[f - 2])
                if f == 0:
                    P.wait("sync", sGo, t_opsw_last)
                    P.wait("sync", sV, t_arB_v)
                if first:
                    P.wait("sync", sCa, 16 * min(9 * (f // 9 + 1), NFT))
                tload = P.op("sync", lambda e, f=f, w=w: e.dma_start(
                    out=w1w[f % 2][:, :, 0:w],
                    in_=w1t[f, :, :, 0:w]), sWw, 16)
                P.wait("tensor", sWw, tload)
                if f == 0:
                    P.wait("tensor", sV, t_colT_ready)
                wait_bank("tensor", "A%d" % (f % 2))
                for k in range(16):
                    fn = (lambda e, f=f, k=k, w=w: e.matmul(
                        out=bankA(f % 2)[0:w, :], lhsT=w1w[f % 2][:, k, 0:w],
                        rhs=colT[:, k, :], start=(k == 0), stop=(k == 15)))
                    if k == 15:
                        t_mm[f] = P.op("tensor", fn, sT)
                    else:
                        P.op("tensor", fn)
                P.wait("scalar", sT, t_mm[f])
                if f < 32:
                    P.wait("scalar", sGo, t_xsil_dma[f % 2])
                    tz = P.op("scalar", lambda e, f=f: e.activation(
                        out=xsil[:, f % 2, :], in_=bankA(f % 2), func=AF.Silu, scale=0.25), sS)
                    bank_free["A%d" % (f % 2)] = (sS, tz)
                    P.wait("gpsimd", sS, tz)
                    t_xsil_dma[f % 2] = P.op("gpsimd", lambda e, f=f: e.dma_start(
                        out=szd[f * 128:(f + 1) * 128, :], in_=xsil[:, f % 2, :]), sGo, 16)
                elif f < 66:
                    ct = f - 32
                    P.wait("scalar", sV, t_conv_v[f % 2])
                    tc = P.op("scalar", lambda e, f=f: e.activation(
                        out=xbcp[:, f % 2, 3:515], in_=bankA(f % 2), func=AF.Copy, scale=0.5), sS)
                    bank_free["A%d" % (f % 2)] = (sS, tc)
                    P.wait("vector", sS, tc)
                    th0 = P.op("vector", lambda e, s=s, ct=ct, f=f: e.tensor_copy(
                        out=xbcp[:, f % 2, 0:3], in_=convh[:, s, ct, :]), sV)
                    P.wait("vector", sV, th0)
                    tcv0 = P.op("vector", lambda e, f=f, ct=ct: e.tensor_scalar_mul(
                        out=cacc[:, f % 2, :], in0=xbcp[:, f % 2, 0:512],
                        scalar1=c_convw[:, ct, 0:1]), sV)
                    for k in range(1, 4):
                        P.wait("vector", sV, tcv0)
                        tcv0 = P.op("vector", lambda e, f=f, ct=ct, k=k: e.scalar_tensor_tensor(
                            out=cacc[:, f % 2, :], in0=xbcp[:, f % 2, k:k + 512],
                            scalar=c_convw[:, ct, k:k + 1], in1=cacc[:, f % 2, :],
                            op0=ALU.mult, op1=ALU.add), sV)
                    P.wait("vector", sV, tcv0)
                    t_conv_v[f % 2] = P.op("vector", lambda e, s=s, ct=ct, f=f: e.tensor_copy(
                        out=convh[:, s, ct, :], in_=xbcp[:, f % 2, 512:515]), sV)
                    P.wait("scalar", sV, t_conv_v[f % 2])
                    if ct < 32:
                        P.wait("scalar", sGo, t_xsil_dma[f % 2])
                        tsl = P.op("scalar", lambda e, f=f, ct=ct: e.activation(
                            out=xsil[:, f % 2, :], in_=cacc[:, f % 2, :], func=AF.Silu,
                            bias=c_convb[:, ct:ct + 1]), sS)
                        P.wait("gpsimd", sS, tsl)
                        t_xsil_dma[f % 2] = P.op("gpsimd", lambda e, ct=ct, f=f: e.dma_start(
                            out=xbd[ct * 128:(ct + 1) * 128, :], in_=xsil[:, f % 2, :]), sGo, 16)
                    elif ct == 32:
                        P.wait("scalar", sT, pc.get("gmm_last", 0))
                        P.op("scalar", lambda e, f=f, ct=ct: e.activation(
                            out=Bb, in_=cacc[:, f % 2, :], func=AF.Silu,
                            bias=c_convb[:, ct:ct + 1]), sS)
                    else:
                        P.op("scalar", lambda e, f=f, ct=ct: e.activation(
                            out=Cb, in_=cacc[:, f % 2, :], func=AF.Silu,
                            bias=c_convb[:, ct:ct + 1]), sS)
                else:
                    P.wait("vector", sV, t_arB_v)
                    P.wait("vector", sT, t_mm[f])
                    td0 = P.op("vector", lambda e, f=f: e.tensor_scalar_add(
                        out=dt_t1, in0=bankA(f % 2)[0:64, :], scalar1=c_dtb), sV)
                    bank_free["A%d" % (f % 2)] = (sV, td0)
                    P.wait("vector", sV, td0)
                    tda = P.op("vector", lambda e: e.tensor_scalar_mul(
                        out=dt_t2, in0=dt_t1, scalar1=-1.0), sV)
                    P.wait("vector", sV, tda)
                    tv = P.op("vector", lambda e: e.tensor_tensor(
                        out=dt_t2, in0=dt_t1, in1=dt_t2, op=ALU.min), sV)
                    P.wait("scalar", sV, tv)
                    tse = P.op("scalar", lambda e: e.activation(out=dt_t3, in_=dt_t2, func=AF.Exp), sS)
                    P.wait("scalar", sS, tse)
                    ts2 = P.op("scalar", lambda e: e.activation(
                        out=dt_t3, in_=dt_t3, func=AF.Ln, bias=1.0), sS)
                    P.wait("vector", sS, ts2)
                    P.wait("vector", sV, tv)
                    tdb = P.op("vector", lambda e: e.tensor_scalar_max(
                        out=dt_t2, in0=dt_t1, scalar1=0.0), sV)
                    P.wait("vector", sV, tdb)
                    t_dt = P.op("vector", lambda e: e.tensor_add(out=dtF, in0=dt_t2, in1=dt_t3), sV)
                    t_arB_v = t_dt
            t_colT_consumed = t_mm[NFT - 1]
            t_BC_scalar = sS.val
            t_B_gp = sGo.val

            if last_step and PH < 2:
                continue
            P.wait("vector", sT, pc.get("t_cbt", 0))
            P.wait("vector", sGs, pc.get("tgw", 0))
            P.wait("vector", sV, t_dt)
            tscan = P.op("vector", lambda e: e.tensor_tensor_scan(
                out=cumF, data0=c_ones64.broadcast_to([64, 512]), data1=dtF,
                initial=0.0, op0=ALU.mult, op1=ALU.add), sV)
            P.wait("vector", sV, tscan)
            tlast = P.op("vector", lambda e: e.tensor_scalar_mul(
                out=GinF[:, 0:128], in0=cumF[:, 0:128], scalar1=c_acol), sV)
            for c in range(1, NCH):
                tlast = P.op("vector", lambda e, c=c: e.tensor_scalar(
                    out=GinF[:, c * 128:(c + 1) * 128],
                    in0=cumF[:, c * 128:(c + 1) * 128],
                    scalar1=cumF[:, c * 128 - 1:c * 128], scalar2=c_acol,
                    op0=ALU.subtract, op1=ALU.mult), sV)
            t_inarg = tlast
            for c in range(NCH):
                tlast = P.op("vector", lambda e, c=c: e.tensor_scalar(
                    out=dEndArgF[:, c * 128:(c + 1) * 128],
                    in0=cumF[:, c * 128:(c + 1) * 128],
                    scalar1=cumF[:, c * 128 + 127:c * 128 + 128], scalar2=c_eacol,
                    op0=ALU.subtract, op1=ALU.mult), sV)
            t_endarg = tlast
            P.wait("gpsimd", sV, t_inarg)
            tgw = P.op("gpsimd", lambda e: e.dma_start(
                out=gsc3.rearrange("c (h t) -> h c t", t=128),
                in_=GinF.rearrange("h (c t) -> h c t", t=128)), sGs, 16)
            pc["tgw"] = tgw
            for c in range(NCH):
                cp = c % 2
                P.wait("sync", sGo, t_B_gp)
                if c == 0:
                    P.wait("sync", sT, t_hbuf_cons)
                wait_bank("sync", "XT%d" % cp)
                t_xt = P.op("sync", lambda e, c=c, cp=cp: e.dma_start(
                    out=XTc[:, cp, :], in_=xbd[:, c * 128:(c + 1) * 128], transpose=True), sWt, 16)
                t_xt_reads = sWt.val
                wait_bank("tensor", "B2", "B3")
                P.wait("tensor", sV, t_endarg)
                P.op("tensor", lambda e, c=c: e.transpose(
                    out=ps_gt, in_=GinF[:, c * 128:(c + 1) * 128], identity=c_idf[0:64, 0:64]))
                P.op("tensor", lambda e, c=c: e.transpose(
                    out=ps_dendt, in_=dEndArgF[:, c * 128:(c + 1) * 128],
                    identity=c_idf[0:64, 0:64]))
                P.op("tensor", lambda e, c=c: e.transpose(
                    out=ps_dtt, in_=dtF[:, c * 128:(c + 1) * 128], identity=c_idf[0:64, 0:64]))
                P.wait("tensor", sS, t_BC_scalar)
                P.op("tensor", lambda e, c=c: e.transpose(
                    out=ps_bt, in_=Bb[:, c * 128:(c + 1) * 128], identity=c_idb))
                t_cbt = P.op("tensor", lambda e, c=c: e.matmul(
                    out=ps_cbt, lhsT=Bb[:, c * 128:(c + 1) * 128],
                    rhs=Cb[:, c * 128:(c + 1) * 128], start=True, stop=True), sT)
                P.wait("vector", sT, t_cbt)
                P.wait("vector", sT, pc.get("gmm_last", 0))
                P.wait("vector", sS, pc.get("texpi", 0))
                P.op("vector", lambda e: e.tensor_copy(out=GT, in_=ps_gt))
                P.op("vector", lambda e: e.tensor_copy(out=dEndT, in_=ps_dendt))
                P.op("vector", lambda e: e.tensor_copy(out=dtT, in_=ps_dtt))
                P.op("vector", lambda e: e.tensor_copy(out=BTb, in_=ps_bt))
                t_sm = P.op("vector", lambda e: e.tensor_mul(out=CBm, in0=ps_cbt, in1=c_umask), sV)
                bank_free["B2"] = (sV, t_sm)
                bank_free["B3"] = (sV, t_sm)
                P.wait("scalar", sV, t_sm)
                P.wait("scalar", sGs, pc.get("tgam_c", 0))
                texpi = P.op("scalar", lambda e: e.activation(
                    out=dInT, in_=GT, func=AF.Exp), sS)
                pc["texpi"] = texpi
                texpe = P.op("scalar", lambda e: e.activation(
                    out=dEndT, in_=dEndT, func=AF.Exp), sS)
                P.wait("vector", sS, texpe)
                twE = P.op("vector", lambda e: e.tensor_mul(out=dEndT, in0=dEndT, in1=dtT), sV)
                P.wait("gpsimd", sS, texpi)
                P.wait("gpsimd", sV, t_arB_v)
                tgam_c = P.op("gpsimd", lambda e, c=c: e.dma_start(
                    out=gamdF[c:c + 1, :], in_=dInT[127:128, :]), sGs, 16)
                pc["tgam_c"] = tgam_c
                tgby = P.op("gpsimd", lambda e, c=c: e.dma_start(
                    out=gby, in_=gamdF[c:c + 1, :].partition_broadcast(128).squeeze(1)), sGs, 16)
                P.wait("vector", sV, twE)

                t_group_v = {}
                t_gmm = 0
                if c == 0:
                    P.wait("vector", sS, t_BC_scalar)
                    P.wait("vector", sT, t_hbuf_cons)
                P.wait("vector", sT, pc.get("ytr", 0))
                for g in range(NG):
                    gp = g % 2
                    P.wait("gpsimd", sGs, tgby)
                    if g >= 2:
                        P.wait("gpsimd", sV, t_group_v[g - 2])
                    else:
                        P.wait("gpsimd", sV, t_arB_v)
                    t_grp = P.op("gpsimd", lambda e, c=c, g=g, gp=gp: e.dma_start(
                        out=grpv[gp],
                        in_=gsc3[c:c + 1, g * 1024:(g + 1) * 1024]
                        .partition_broadcast(128).squeeze(1)), sGr, 16)
                    P.wait("vector", sGr, t_grp)
                    P.wait("vector", sV, t_sm)
                    tdb0 = P.op("vector", lambda e, gp=gp, g=g: e.tensor_tensor(
                        out=dbufv.rearrange("p (h t) -> p h t", h=8),
                        in0=grpv[gp].rearrange("p (h t) -> p h t", h=8),
                        in1=GT[:, g * 8:(g + 1) * 8].unsqueeze(2).broadcast_to([128, 8, 128]),
                        op=ALU.subtract), sV)
                    P.wait("vector", sV, tdb0)
                    tdm = P.op("vector", lambda e, gp=gp: e.tensor_scalar_min(
                        out=dminv[gp], in0=dbufv, scalar1=0.0), sV)
                    P.wait("scalar", sV, tdm)
                    texp = P.op("scalar", lambda e, gp=gp: e.activation(
                        out=expdv[gp], in_=dminv[gp], func=AF.Exp), sS)
                    P.wait("vector", sS, texp)
                    P.op("vector", lambda e, gp=gp: e.tensor_tensor(
                        out=mtb[:, gp, :, :],
                        in0=expdv[gp].rearrange("p (h t) -> p h t", h=8),
                        in1=CBm.unsqueeze(1).broadcast_to([128, 8, 128]), op=ALU.mult))
                    if g == 0:
                        P.wait("vector", sWt, t_xt)
                    xtv = XTc[:, cp, :].rearrange("p (h q) -> p h q", h=64)
                    P.op("vector", lambda e, gp=gp, g=g, xtv=xtv: e.tensor_tensor(
                        out=xdt[:, gp, :, :], in0=xtv[:, g * 8:(g + 1) * 8, :],
                        in1=dtT[:, g * 8:(g + 1) * 8].unsqueeze(2).broadcast_to([128, 8, 64]),
                        op=ALU.mult))
                    P.op("vector", lambda e, gp=gp, g=g, xtv=xtv: e.tensor_tensor(
                        out=xwv[:, gp, :, :], in0=xtv[:, g * 8:(g + 1) * 8, :],
                        in1=dEndT[:, g * 8:(g + 1) * 8].unsqueeze(2).broadcast_to([128, 8, 64]),
                        op=ALU.mult))
                    tprep = P.op("vector", lambda e, gp=gp, g=g, xtv=xtv: e.tensor_tensor(
                        out=dxt[:, gp, :, :], in0=xtv[:, g * 8:(g + 1) * 8, :],
                        in1=c_dbc[:, g * 8:(g + 1) * 8].unsqueeze(2).broadcast_to([128, 8, 64]),
                        op=ALU.mult), sV)
                    P.wait("tensor", sV, tprep)
                    wait_bank("tensor", "A%d" % gp, "A%d" % (2 + gp), "B%d" % gp)
                    for j in range(8):
                        P.op("tensor", lambda e, c=c, g=g, j=j, gp=gp: e.matmul(
                            out=bankA(2 + gp)[:, j * 64:(j + 1) * 64],
                            lhsT=Cb[:, c * 128:(c + 1) * 128],
                            rhs=Hbf[:, (8 * g + j) * 64:(8 * g + j + 1) * 64],
                            start=(j == 0), stop=(j == 7)))
                    for j in range(8):
                        P.op("tensor", lambda e, g=g, j=j, gp=gp: e.matmul(
                            out=bankA(gp)[:, j * 64:(j + 1) * 64],
                            lhsT=mtb[:, gp, j, :], rhs=xdt[:, gp, j, :],
                            start=(j == 0), stop=(j == 7)))
                    for j in range(8):
                        fn = (lambda e, g=g, j=j, gp=gp: e.matmul(
                            out=bankB(gp)[:, j * 64:(j + 1) * 64],
                            lhsT=BTb, rhs=xwv[:, gp, j, :],
                            start=(j == 0), stop=(j == 7)))
                        if j == 7:
                            t_gmm = P.op("tensor", fn, sT)
                        else:
                            P.op("tensor", fn)
                    P.wait("vector", sT, t_gmm)
                    if g == 0:
                        P.wait("vector", sGs, tgby)
                    ysl = Ybuf[:, g * 512:(g + 1) * 512].rearrange("p (h q) -> p h q", h=8)
                    ty1 = P.op("vector", lambda e, g=g, gp=gp, ysl=ysl: e.tensor_tensor(
                        out=ysl, in0=bankA(2 + gp).rearrange("p (h q) -> p h q", h=8),
                        in1=dInT[:, g * 8:(g + 1) * 8].unsqueeze(2).broadcast_to([128, 8, 64]),
                        op=ALU.mult), sV)
                    P.wait("vector", sV, ty1)
                    ty2 = P.op("vector", lambda e, gp=gp, ysl=ysl: e.tensor_tensor(
                        out=ysl, in0=ysl,
                        in1=bankA(gp).rearrange("p (h q) -> p h q", h=8), op=ALU.add), sV)
                    P.wait("vector", sV, ty2)
                    P.op("vector", lambda e, gp=gp, ysl=ysl: e.tensor_tensor(
                        out=ysl, in0=ysl, in1=dxt[:, gp, :, :], op=ALU.add))
                    hsl = Hb[:, g * 512:(g + 1) * 512].rearrange("p (h q) -> p h q", h=8)
                    th1 = P.op("vector", lambda e, g=g, hsl=hsl: e.tensor_tensor(
                        out=hsl, in0=hsl,
                        in1=gby[:, g * 8:(g + 1) * 8].unsqueeze(2).broadcast_to([128, 8, 64]),
                        op=ALU.mult), sV)
                    P.wait("vector", sV, th1)
                    th2 = P.op("vector", lambda e, gp=gp, hsl=hsl: e.tensor_tensor(
                        out=hsl, in0=hsl,
                        in1=bankB(gp).rearrange("p (h q) -> p h q", h=8), op=ALU.add), sV)
                    P.wait("vector", sV, th2)
                    t_group_v[g] = P.op("vector", lambda e, g=g: e.tensor_copy(
                        out=Hbf[:, g * 512:(g + 1) * 512],
                        in_=Hb[:, g * 512:(g + 1) * 512]), sV)
                    bank_free["A%d" % gp] = (sV, t_group_v[g])
                    bank_free["A%d" % (2 + gp)] = (sV, t_group_v[g])
                    bank_free["B%d" % gp] = (sV, t_group_v[g])
                bank_free["XT%d" % cp] = (sV, t_group_v[NG - 1])
                t_arB_v = t_group_v[NG - 1]
                pc.update(t_cbt=t_cbt, tgby=tgby, gmm_last=t_gmm)

                P.wait("vector", sV, t_group_v[NG - 1])
                tsq0 = P.op("vector", lambda e: e.scalar_tensor_tensor(
                    out=sqdump, in0=Ybuf, scalar=1.0, in1=Ybuf,
                    op0=ALU.mult, op1=ALU.mult, accum_out=ssq), sV)
                P.wait("vector", sV, tsq0)
                t_sq = P.op("vector", lambda e: e.tensor_scalar(
                    out=lnvar, in0=ssq, scalar1=1.0 / INTER, scalar2=EPS,
                    op0=ALU.mult, op1=ALU.add), sV)
                P.wait("scalar", sV, t_sq)
                tgl0 = P.op("scalar", lambda e: e.activation(out=lnln, in_=lnvar, func=AF.Ln), sS)
                P.wait("scalar", sS, tgl0)
                trst = P.op("scalar", lambda e: e.activation(
                    out=grstd, in_=lnln, func=AF.Exp, scale=-0.5), sS)
                P.wait("vector", sS, trst)
                tys = P.op("vector", lambda e: e.tensor_scalar_mul(
                    out=ysb, in0=Ybuf, scalar1=grstd), sV)
                t_gating_v = tys
                t_arB_v = tys
                P.wait("tensor", sV, tys)
                if c == 0:
                    P.wait("scalar", sT, t_hbuf_cons)
                t_ycp = 0
                for r in range(4):
                    srcs = [ysb[:, (r * 8 + j) * 128:(r * 8 + j + 1) * 128]
                            for j in range(8)]
                    pre_cp = []
                    if r == 0:
                        pre_cp.append(("scalar", sGo, pc.get("t_ybt_dma", 0)))
                    ltr, t_ycp = tr_batch(srcs, "scalar", sS,
                                          dst2d=ybtstg[:, r * 1024:(r + 1) * 1024],
                                          pre_cp=pre_cp)
                pc["ytr"] = ltr
                P.wait("gpsimd", sS, t_ycp)
                if c == 0:
                    P.wait("gpsimd", sWt, t_yb_reads)
                pc["t_ybt_dma"] = P.op("gpsimd", lambda e, c=c: e.dma_start(
                    out=ybdT[:, :, c, :].transpose([1, 0, 2]),
                    in_=ybtstg.rearrange("p (j n) -> p j n", j=32)), sGo, 16)
            t_yb_done = sGo.val

            if last_step and PH < 3:
                continue
            t_ops_w = 0
            t_glv_prev_last = 0
            t_mmfin_prev = 0
            for dh in range(2):
                t_glcons = {}
                t_glv = {}
                t_mmfin = 0
                for j in range(32):
                    jp = j % 4
                    if j % 2 == 0:
                        if j == 0 and dh == 0:
                            P.wait("sync", sGo, t_yb_done)
                            if first:
                                P.wait("sync", sCo, t_wout_cast)
                        if j < 4 and dh == 1:
                            P.wait("sync", sV, t_glv_prev_last)
                            P.wait("sync", sT, t_mmfin_prev)
                        if j >= 4:
                            P.wait("sync", sV, t_glv[j - 3])
                            P.wait("sync", sT, t_glcons[j - 3])
                        tg1 = P.op("sync", lambda e, j=j, jp=jp: e.dma_start(
                            out=glraw[:, jp:jp + 2, :],
                            in_=ybdT[j:j + 2].rearrange("j p c t -> p j (c t)")), sWt, 16)
                        P.op("sync", lambda e, j=j, jp=jp: e.dma_start(
                            out=szw[:, jp:jp + 2, :],
                            in_=szd[j * 128:(j + 2) * 128, :]
                            .rearrange("(j p) t -> p j t", p=128)), sWw, 16)
                        tg3 = P.op("sync", lambda e, j=j, jp=jp, dh=dh: e.dma_start(
                            out=wow4v[:, jp:jp + 2, :],
                            in_=woutb[j * 128:(j + 2) * 128, dh * 1024:(dh + 1) * 1024]
                            .rearrange("(j p) n -> p j n", p=128)), sWw, 16)
                        tg1_pair, tg3_pair = tg1, tg3
                    P.wait("vector", sWt, tg1_pair)
                    P.wait("vector", sWw, tg3_pair)
                    if j >= 4:
                        P.wait("vector", sT, t_glcons[j - 4])
                    elif dh == 1:
                        P.wait("vector", sT, t_mmfin_prev)
                    t_glv[j] = P.op("vector", lambda e, j=j, jp=jp: e.scalar_tensor_tensor(
                        out=glraw[:, jp, :], in0=glraw[:, jp, :],
                        scalar=c_normw[:, j:j + 1], in1=szw[:, jp, :],
                        op0=ALU.mult, op1=ALU.mult), sV)
                    P.wait("tensor", sV, t_glv[j])
                    if j == 0:
                        wait_bank("tensor", *["A%d" % b for b in range(4)],
                                  *["B%d" % b for b in range(4)])
                    for tti in range(4):
                        for dq in range(2):
                            bk = bankA(tti) if dq == 0 else bankB(tti)
                            fn = (lambda e, j=j, jp=jp, tti=tti, dq=dq, bk=bk: e.matmul(
                                out=bk, lhsT=glraw[:, jp, tti * 128:(tti + 1) * 128],
                                rhs=wow4v[:, jp, dq * 512:(dq + 1) * 512],
                                start=(j == 0), stop=(j == 31)))
                            if tti == 3 and dq == 1:
                                t_glcons[j] = P.op("tensor", fn, sT)
                                if j == 31:
                                    t_mmfin = t_glcons[j]
                            else:
                                P.op("tensor", fn)
                t_glv_prev_last = t_glv[31]
                t_mmfin_prev = t_mmfin
                t_yb_reads = sWt.val
                t_sz_reads = sWw.val
                for tti in range(4):
                    for dq in range(2):
                        bk = bankA(tti) if dq == 0 else bankB(tti)
                        P.wait("vector", sT, t_mmfin)
                        if dh == 0 and tti == 0 and dq == 0:
                            P.wait("vector", sGo, t_yb_done)
                        te = P.op("vector", lambda e, bk=bk, tti=tti, dh=dh, dq=dq:
                                  e.tensor_copy(
                                      out=aosb[:, tti * 2048 + dh * 1024 + dq * 512:
                                               tti * 2048 + dh * 1024 + (dq + 1) * 512],
                                      in_=bk), sV)
                        bank_free[("A%d" if dq == 0 else "B%d") % tti] = (sV, te)
            t_x1w = 0
            for tt in range(4):
                P.wait("gpsimd", sV, t_arB_v)
                if tt == 0:
                    P.wait("gpsimd", sGa, t_x1_reads)
                tx = P.op("gpsimd", lambda e, s=s, r0=t0 + tt * 128: e.dma_start(
                    out=xtileB, in_=x_in[s, r0:r0 + 128, :]), sGa, 16)
                if t_x1w:
                    P.wait("vector", sGo, t_x1w)
                trs = ln_stats(aosb[:, tt * 2048:(tt + 1) * 2048], HID)
                P.wait("vector", sS, trs)
                P.wait("vector", sGa, tx)
                tn1 = P.op("vector", lambda e, tt=tt: e.tensor_scalar(
                    out=aosb[:, tt * 2048:(tt + 1) * 2048],
                    in0=aosb[:, tt * 2048:(tt + 1) * 2048],
                    scalar1=lnm, scalar2=lnrstd,
                    op0=ALU.subtract, op1=ALU.mult), sV)
                P.wait("vector", sV, tn1)
                tx1 = P.op("vector", lambda e, tt=tt: e.tensor_add(
                    out=aosb[:, tt * 2048:(tt + 1) * 2048],
                    in0=aosb[:, tt * 2048:(tt + 1) * 2048], in1=xtileB), sV)
                t_arB_v = tx1
                P.wait("gpsimd", sV, tx1)
                t_x1w = P.op("gpsimd", lambda e, tt=tt: e.dma_start(
                    out=x1b[tt * 128:(tt + 1) * 128, :],
                    in_=aosb[:, tt * 2048:(tt + 1) * 2048]), sGo, 16)

            if last_step and PH < 4:
                continue
            tr_hist = {}
            for tt in range(4):
                if tt >= 2:
                    P.wait("vector", sT, tr_hist[tt - 2])
                if tt == 0:
                    P.wait("vector", sGo, t_x1w)
                trs = ln_stats(aosb[:, tt * 2048:(tt + 1) * 2048], HID)
                P.wait("vector", sS, trs)
                tn = P.op("vector", lambda e, tt=tt: e.tensor_scalar(
                    out=xnst[:, tt % 2, :], in0=aosb[:, tt * 2048:(tt + 1) * 2048],
                    scalar1=lnm, scalar2=lnrstd,
                    op0=ALU.subtract, op1=ALU.mult), sV)
                t_arB_v = tn
                P.wait("tensor", sV, tn)
                for r in range(2):
                    srcs = [xnst[:, tt % 2, (r * 8 + k) * 128:(r * 8 + k + 1) * 128]
                            for k in range(8)]
                    pre_cp = []
                    if tt == 0 and r == 0:
                        pre_cp.append(("vector", sT, t_colT_consumed))
                    ltr, t_colT_ready = tr_batch(
                        srcs, "vector", sV,
                        dst3d=colT[:, r * 8:(r + 1) * 8, tt * 128:(tt + 1) * 128],
                        pre_cp=pre_cp)
                tr_hist[tt] = ltr
            t_h = {}
            t_gu = {}
            cap_tr = {}
            cap_wo = {}
            for i in range(64):
                ip = i % 2
                if not first:
                    if i == 0:
                        P.wait("sync", sT, t_mmfin)
                        P.wait("sync", sGo, t_opsw_last)
                        if s == 0 and sc == 1:
                            P.wait("sync", sGo, t_cap_gu_done)
                    if i >= 2:
                        P.wait("sync", sT, t_gu[i - 2])
                    tga = P.op("sync", lambda e, i=i, ip=ip: e.dma_start(
                        out=wguw[ip][0], in_=wgutc[i]), sWw, 16)
                    tgb = P.op("sync", lambda e, i=i, ip=ip: e.dma_start(
                        out=wguw[ip][1], in_=wgutc[i + 64]), sWw, 16)
                    P.wait("tensor", sWw, tgb)
                else:
                    for half in (0, 1):
                        it = i + 64 * half
                        if i >= 1:
                            P.wait("sync", sT, cap_tr[(i - 1, half)])
                        if i == 0 and half == 0:
                            P.wait("sync", sCb, t_wgud_cast)
                        tld = P.op("sync", lambda e, it=it, half=half: e.dma_start(
                            out=stag[half],
                            in_=wgub[it * 128:(it + 1) * 128, :]
                            .rearrange("p (j n) -> p j n", j=16)), sWw, 16)
                        P.wait("tensor", sWw, tld)
                        t_ccp = 0
                        for r in range(2):
                            srcs = [stag[half][:, r * 8 + j, :] for j in range(8)]
                            pre_cp = []
                            if r == 0:
                                if i >= 2:
                                    pre_cp += [("vector", sT, t_gu[i - 2]),
                                               ("vector", sGo, cap_wo[(i - 2, half)])]
                                else:
                                    pre_cp += [("vector", sT, t_mmfin),
                                               ("vector", sGo, t_opsw_last)]
                            ltr, t_ccp = tr_batch(
                                srcs, "vector", sV,
                                dst2d=wguwF[ip][half][:, r * 1024:(r + 1) * 1024],
                                pre_cp=pre_cp)
                        cap_tr[(i, half)] = ltr
                        P.wait("gpsimd", sV, t_ccp)
                        cap_wo[(i, half)] = P.op("gpsimd", lambda e, it=it, ip=ip, half=half:
                                                 e.dma_start(out=wgutc[it],
                                                             in_=wguw[ip][half]), sGo, 16)
                    t_cap_gu_done = cap_wo[(63, 1)] if i == 63 else t_cap_gu_done
                    P.wait("tensor", sV, t_ccp)
                if i == 0:
                    P.wait("tensor", sV, t_colT_ready)
                wait_bank("tensor", "A%d" % ip, "A%d" % (2 + ip))
                for k in range(16):
                    P.op("tensor", lambda e, ip=ip, k=k: e.matmul(
                        out=bankA(ip), lhsT=wguw[ip][0][:, k, :], rhs=colT[:, k, :],
                        start=(k == 0), stop=(k == 15)))
                for k in range(16):
                    fn = (lambda e, ip=ip, k=k: e.matmul(
                        out=bankA(2 + ip), lhsT=wguw[ip][1][:, k, :], rhs=colT[:, k, :],
                        start=(k == 0), stop=(k == 15)))
                    if k == 15:
                        t_gu[i] = P.op("tensor", fn, sT)
                    else:
                        P.op("tensor", fn)
                P.wait("scalar", sT, t_gu[i])
                if i >= 2:
                    P.wait("scalar", sV, t_h[i - 2])
                tsg = P.op("scalar", lambda e, ip=ip: e.activation(
                    out=sgst[:, ip, :], in_=bankA(ip), func=AF.Silu), sS)
                bank_free["A%d" % ip] = (sS, tsg)
                P.wait("vector", sS, tsg)
                if i == 0:
                    P.wait("vector", sT, t_hbuf_cons)
                t_h[i] = P.op("vector", lambda e, i=i, ip=ip: e.tensor_tensor(
                    out=hbuf[:, i, :], in0=sgst[:, ip, :], in1=bankA(2 + ip),
                    op=ALU.mult), sV)
                bank_free["A%d" % (2 + ip)] = (sV, t_h[i])
            t_colT_consumed = t_gu[63]
            t_dn = {}
            t_dn_half = {}
            t_ff_w = 0
            t_fftr = {}
            t_ffw = {}
            cap_dtr = {}
            cap_dwo = {}
            for d in range(16):
                dp = d % 2
                for qr in range(4):
                    hb = (4 * d + qr) % 2
                    q_idx = 4 * d + qr
                    if not first:
                        if q_idx == 0:
                            P.wait("sync", sT, t_hbuf_cons)
                            if s == 0 and sc == 1:
                                P.wait("sync", sGo, t_cap_dn_done)
                        if q_idx >= 2:
                            P.wait("sync", sT, t_dn_half[q_idx - 2])
                        twd = P.op("sync", lambda e, d=d, qr=qr, hb=hb: e.dma_start(
                            out=wdw[hb], in_=wdtc[d, :, qr * 16:(qr + 1) * 16, :]), sWw, 16)
                        P.wait("tensor", sWw, twd)
                    else:
                        if q_idx >= 2:
                            P.wait("sync", sT, cap_dtr[q_idx - 2])
                        else:
                            P.wait("sync", sT, cap_tr[(63, hb)])
                        tld = P.op("sync", lambda e, d=d, qr=qr: e.dma_start(
                            out=stag[(4 * d + qr) % 2],
                            in_=wdb[d * 128:(d + 1) * 128, qr * 2048:(qr + 1) * 2048]
                            .rearrange("p (j n) -> p j n", j=16)), sWw, 16)
                        P.wait("tensor", sWw, tld)
                        t_ccp = 0
                        for r in range(2):
                            srcs = [stag[hb][:, r * 8 + j, :] for j in range(8)]
                            pre_cp = []
                            if r == 0:
                                if q_idx >= 2:
                                    pre_cp += [("vector", sT, t_dn_half[q_idx - 2]),
                                               ("vector", sGo, cap_dwo[q_idx - 2])]
                                else:
                                    pre_cp += [("vector", sT, t_hbuf_cons)]
                            ltr, t_ccp = tr_batch(
                                srcs, "vector", sV,
                                dst2d=wdwF[hb][:, r * 1024:(r + 1) * 1024],
                                pre_cp=pre_cp)
                        cap_dtr[q_idx] = ltr
                        P.wait("gpsimd", sV, t_ccp)
                        cap_dwo[q_idx] = P.op("gpsimd", lambda e, d=d, qr=qr, hb=hb:
                                              e.dma_start(out=wdtc[d, :, qr * 16:(qr + 1) * 16, :],
                                                          in_=wdw[hb]), sGo, 16)
                        if q_idx == 63:
                            t_cap_dn_done = cap_dwo[q_idx]
                        P.wait("tensor", sV, t_ccp)
                    if d == 0 and qr == 0:
                        P.wait("tensor", sV, t_h[63])
                        wait_bank("tensor", "B0", "B1")
                    if qr == 0:
                        wait_bank("tensor", "B%d" % dp)
                    for ii in range(16):
                        iglob = qr * 16 + ii
                        fn = (lambda e, d=d, dp=dp, hb=hb, ii=ii, iglob=iglob: e.matmul(
                            out=bankB(dp), lhsT=wdw[hb][:, ii, :], rhs=hbuf[:, iglob, :],
                            start=(iglob == 0), stop=(iglob == 63)))
                        if ii == 15:
                            tlast = P.op("tensor", fn, sT)
                            t_dn_half[4 * d + qr] = tlast
                            if qr == 3:
                                t_dn[d] = tlast
                        else:
                            P.op("tensor", fn)
                P.wait("vector", sT, t_dn[d])
                P.wait("vector", sT, t_fftr.get(dp, 0))
                tfe = P.op("vector", lambda e, d=d, dp=dp: e.tensor_copy(
                    out=ffst[:, dp, :], in_=bankB(dp)), sV)
                bank_free["B%d" % dp] = (sV, tfe)
                P.wait("tensor", sV, tfe)
                pre_cp = [("scalar", sGo, t_ffw.get(dp, 0))]
                if first and d == 0:
                    pre_cp.append(("scalar", sGo, t_cap_gu_done))
                srcs = [ffst[:, dp, tt * 128:(tt + 1) * 128] for tt in range(4)]
                ltr, tcp = tr_batch(srcs, "scalar", sS, dst2d=stagF[dp], pre_cp=pre_cp)
                t_fftr[dp] = ltr
                P.wait("gpsimd", sS, tcp)
                t_ff_w = P.op("gpsimd", lambda e, d=d, dp=dp: e.dma_start(
                    out=ffbdT[:, :, d, :].transpose([1, 0, 2]),
                    in_=stagF[dp].rearrange("p (c n) -> p c n", c=4)), sGo, 16)
                t_ffw[dp] = t_ff_w
            t_hbuf_cons = t_dn[15]
            t_ln4v = {}
            for tt in range(4):
                ftp = tt % 2
                P.wait("sync", sGo, t_ff_w)
                if tt >= 2:
                    P.wait("sync", sV, t_ln4v[tt - 2])
                t_fft = P.op("sync", lambda e, tt=tt, ftp=ftp: e.dma_start(
                    out=fftl[ftp], in_=ffbdT[tt].rearrange("p d n -> p (d n)")), sWt, 16)
                P.wait("gpsimd", sV, t_arB_v)
                tx = P.op("gpsimd", lambda e, tt=tt: e.dma_start(
                    out=xtileA, in_=x1b[tt * 128:(tt + 1) * 128, :]), sGa, 16)
                P.wait("vector", sWt, t_fft)
                if t_yout_last:
                    P.wait("vector", sGo, t_yout_last)
                trs = ln_stats(fftl[ftp], HID)
                P.wait("vector", sS, trs)
                P.wait("vector", sGa, tx)
                tn1 = P.op("vector", lambda e, ftp=ftp: e.tensor_scalar(
                    out=outstg[:, 0:2048], in0=fftl[ftp], scalar1=lnm, scalar2=lnrstd,
                    op0=ALU.subtract, op1=ALU.mult), sV)
                P.wait("vector", sV, tn1)
                t_ln4v[tt] = P.op("vector", lambda e: e.tensor_add(
                    out=outstg[:, 0:2048], in0=outstg[:, 0:2048], in1=xtileA), sV)
                t_arB_v = t_ln4v[tt]
                P.wait("gpsimd", sV, t_ln4v[tt])
                t_yout_last = P.op("gpsimd", lambda e, s=s, r0=t0 + tt * 128: e.dma_start(
                    out=y_out[s, r0:r0 + 128, :], in_=outstg[:, 0:2048]), sGo, 16)
            t_x1_reads = sGa.val

    P.wait("gpsimd", sGo, sGo.val)
    P.wait("gpsimd", sGa, sGa.val)
    P.wait("gpsimd", sGr, sGr.val)
    P.wait("gpsimd", sGs, sGs.val)
    P.wait("gpsimd", sCa, sCa.val)
    P.wait("gpsimd", sCo, sCo.val)
    P.wait("gpsimd", sCb, sCb.val)
    P.wait("sync", sWw, sWw.val)
    P.wait("sync", sWt, sWt.val)

    blk = ctx.enter_context(nc.Block())
    for eng in ("sync", "gpsimd", "scalar", "vector", "tensor"):
        def runner(e, ops=tuple(P.q[eng])):
            for f in ops:
                f(e)
        getattr(blk, eng)(runner)
    ctx.close()
    return nc


_NC_CACHE = None


def kernel(**inputs):
    global _NC_CACHE
    x = np.ascontiguousarray(inputs["L0_attn_hidden_states"])
    consts = dict(
        idf_i=np.eye(128, dtype=np.float32),
        idb_i=np.eye(128, dtype=ml_dtypes.bfloat16),
        umask_i=np.triu(np.ones((128, 128), np.float32)),
    )
    weights = dict(
        w1=np.ascontiguousarray(inputs["L0_attn_in_proj_w"], dtype=np.float32),
        convw_i=np.ascontiguousarray(inputs["L0_attn_conv_w"], dtype=np.float32),
        convb_i=np.ascontiguousarray(inputs["L0_attn_conv_b"], dtype=np.float32).reshape(34, 128),
        alog_i=np.ascontiguousarray(inputs["L0_attn_A_log"], dtype=np.float32).reshape(1, 64),
        dvec_i=np.ascontiguousarray(inputs["L0_attn_D"], dtype=np.float32).reshape(1, 64),
        dtb_i=np.ascontiguousarray(inputs["L0_attn_dt_bias"], dtype=np.float32).reshape(1, 64),
        normw_i=np.ascontiguousarray(inputs["L0_attn_norm_w"], dtype=np.float32).reshape(32, 128),
        wout=np.ascontiguousarray(inputs["L0_attn_out_proj_w"], dtype=np.float32),
        wgu=np.ascontiguousarray(inputs["L0_ffn_w_gate_up"], dtype=np.float32),
        wdn=np.ascontiguousarray(inputs["L0_ffn_w_down"], dtype=np.float32),
    )
    if _NC_CACHE is None:
        _NC_CACHE = build_nc()
    nc = _NC_CACHE
    in_maps = []
    for core in range(NCORES):
        m = dict(consts)
        m.update(weights)
        m["x_in"] = np.ascontiguousarray(x[2 * core:2 * core + 2])
        in_maps.append(m)
    res = run_bass_kernel_spmd(nc, in_maps, core_ids=list(range(NCORES)))
    out = np.empty((16, 2048, HID), np.float32)
    for core in range(NCORES):
        out[2 * core:2 * core + 2] = res.results[core]["y_out"]
    return out
